# version 05 (final)
# speedup vs baseline: 1.3710x; 1.0052x over previous
import numpy as np
import ml_dtypes
import concourse.bass as bass
import concourse.mybir as mybir
from concourse.bass_utils import run_bass_kernel_spmd

F32 = mybir.dt.float32
BF16 = mybir.dt.bfloat16
AF = mybir.ActivationFunctionType
ALU = mybir.AluOpType
AXX = mybir.AxisListType.X

T = 2048
D = 2048
NT = D // 128
CH = 512
NCH = T // CH
NH = 16
HD = 128
SCALE = float(1.0 / np.sqrt(128.0))
EPS = 1e-5


class Sem:
    def __init__(self, nc, name):
        sctx = getattr(nc, "sem_ctx", None) or nc.ctx
        self.h = sctx.enter_context(nc.semaphore(name))
        self.n = 0

    def inc(self, inst, k=1):
        inst.then_inc(self.h, k)
        self.n += k
        return self.n

    def dma(self, inst):
        return self.inc(inst, 16)


def host_rope_tables():
    j = np.arange(HD)
    invf = 1.0 / (10000.0 ** (np.arange(0, HD, 2, dtype=np.float64) / HD))
    ang = np.outer(invf[j % 64], np.arange(T, dtype=np.float64))
    return np.cos(ang).astype(np.float32), np.sin(ang).astype(np.float32)


def host_rotmat():
    m = np.zeros((HD, HD), np.float32)
    i = np.arange(0, HD, 2)
    m[i + 1, i] = -1.0
    m[i, i + 1] = 1.0
    return m


def host_masks():
    m = np.zeros((128, 4 * CH), np.float32)
    qq = np.arange(128)[:, None]
    kk = np.arange(CH)[None, :]
    for v in range(4):
        m[:, v * CH:(v + 1) * CH] = (kk <= 128 * v + qq)
    return m


def attn_host_inputs(x_core, Wqkv, Wout):
    cos, sin = host_rope_tables()
    return {
        "x_t": np.ascontiguousarray(x_core.T),
        "wqkvT": np.ascontiguousarray(Wqkv.T),
        "woutT": np.ascontiguousarray(Wout.T),
        "cosT": cos, "sinT": sin,
        "rotm": host_rotmat(), "maskblk": host_masks(),
        "identm": np.eye(128, dtype=np.float32),
    }


def r3(sb_ap, nt, f):
    return sb_ap.rearrange("p (t f) -> p t f", t=nt, f=f)


def col_view(dram, col0, ncol, nt=NT):
    return dram.ap().rearrange("(t p) f -> p t f", p=128)[:, 0:nt, col0:col0 + ncol]


def build_attn_body(nc, x_d, wqkvT_d, woutT_d, cos_d, sin_d, rot_d, msk_d,
                    idn_d, out_d, xln1_d, phases="LSKVB"):
    ctx = nc.ctx
    gd = Sem(nc, "gd")
    sd = Sem(nc, "sd")
    sLNsq = Sem(nc, "sLNsq")
    sLNq = Sem(nc, "sLNq")
    sLNv = Sem(nc, "sLNv")
    sLNs = Sem(nc, "sLNs")
    sA = Sem(nc, "sA")
    sE = Sem(nc, "sE")
    sR = Sem(nc, "sR")
    sP = Sem(nc, "sP")
    gdw = [Sem(nc, "gdw0"), Sem(nc, "gdw1")]
    gdx = [Sem(nc, "gdx0"), Sem(nc, "gdx1")]
    sS = Sem(nc, "sS")
    sMX = Sem(nc, "sMX")
    sEX = Sem(nc, "sEX")
    sMK = Sem(nc, "sMK")
    sTP = Sem(nc, "sTP")
    sPC = Sem(nc, "sPC")
    sLA = Sem(nc, "sLA")
    sEPv = Sem(nc, "sEPv")
    sEPs = Sem(nc, "sEPs")
    sEPt = Sem(nc, "sEPt")

    K = ctx.enter_context(nc.sbuf_tensor("Ksb", [128, NH * T], BF16))
    V = ctx.enter_context(nc.sbuf_tensor("Vsb", [128, NT * T], BF16))
    cosT = ctx.enter_context(nc.sbuf_tensor("cosT_sb", [128, CH], BF16))
    sinT = ctx.enter_context(nc.sbuf_tensor("sinT_sb", [128, CH], BF16))
    rotm = ctx.enter_context(nc.sbuf_tensor("rotm_sb", [128, HD], BF16))
    identb = ctx.enter_context(nc.sbuf_tensor("identb", [128, 128], BF16))
    maskb = ctx.enter_context(nc.sbuf_tensor("maskb", [128, 4 * CH], BF16))
    ones128 = ctx.enter_context(nc.sbuf_tensor("ones128", [128, 1], BF16))
    onesrow = ctx.enter_context(nc.sbuf_tensor("onesrow", [1, 128], F32))
    eps1 = ctx.enter_context(nc.sbuf_tensor("eps1", [1, 1], F32))
    mxp = ctx.enter_context(nc.sbuf_tensor("mxp", [128, 4], F32))
    mrow = ctx.enter_context(nc.sbuf_tensor("mrow", [128, 1], F32))
    negm = ctx.enter_context(nc.sbuf_tensor("negm", [128, 1], F32))
    linv_rep = ctx.enter_context(nc.sbuf_tensor("linv_rep", [128, CH], F32))

    xbf = ctx.enter_context(nc.sbuf_tensor("xbf", [128, NT * CH], BF16))
    xln = ctx.enter_context(nc.sbuf_tensor("xln", [128, NT * CH], BF16))
    wcol = ctx.enter_context(nc.sbuf_tensor("wcol", [128, 2 * 2048], BF16))
    kraw = ctx.enter_context(nc.sbuf_tensor("kraw", [128, 2 * CH], BF16))
    qch = ctx.enter_context(nc.sbuf_tensor("qch", [128, NH * CH], BF16))
    wvbuf = qch
    att = xbf
    xres2 = ctx.enter_context(nc.sbuf_tensor("xres2", [128, 2 * CH], BF16))
    pexp = ctx.enter_context(nc.sbuf_tensor("pexp", [128, 2 * CH], BF16))
    ptr = ctx.enter_context(nc.sbuf_tensor("ptr", [128, 2 * 128], BF16))
    sqb = ctx.enter_context(nc.sbuf_tensor("sqb", [128, 2 * CH], BF16))

    class _L: pass
    ln = _L()
    ln.m = ctx.enter_context(nc.sbuf_tensor("ln_m", [1, CH], F32))
    ln.mv = ctx.enter_context(nc.sbuf_tensor("ln_mv", [1, CH], F32))
    ln.lv = ln.mv
    ln.rstd = ln.mv
    ln.t1 = ctx.enter_context(nc.sbuf_tensor("ln_t1", [128, CH], F32))
    ta = ln.t1
    tb = linv_rep
    linv_row = ln.m

    P = [ctx.enter_context(nc.psum_tensor(f"P{i}", [128, CH], F32))
         for i in range(8)]

    def emit_ln(x_sb, out_sb, resid_dram=None, out2_sb=None):
        b_sq, b_q = sLNsq.n, sLNq.n
        with nc.Block() as blk:
            @blk.scalar
            def _(s):
                for r in range(NT):
                    buf = sqb[:, (r % 2) * CH:(r % 2) * CH + CH]
                    if r >= 2:
                        s.wait_ge(sLNq.h, b_q + r - 1)
                    sLNsq.inc(s.activation(buf, x_sb[:, r * CH:(r + 1) * CH], AF.Square))
            @blk.tensor
            def _(tns):
                for r in range(NT):
                    tns.matmul(P[0][0:1, :], ones128[:, :], x_sb[:, r * CH:(r + 1) * CH],
                               start=(r == 0), stop=(r == NT - 1))
                for r in range(NT):
                    buf = sqb[:, (r % 2) * CH:(r % 2) * CH + CH]
                    tns.wait_ge(sLNsq.h, b_sq + r + 1)
                    sLNq.inc(tns.matmul(P[1][0:1, :], ones128[:, :], buf,
                                        start=(r == 0), stop=(r == NT - 1)))
        b_v, b_s = sLNv.n, sLNs.n
        with nc.Block() as blk:
            @blk.vector
            def _(v):
                sLNv.inc(v.tensor_scalar_mul(ln.m[:, :], P[0][0:1, :], 1.0 / D))
                v.wait_ge(sLNv.h, b_v + 1)
                sLNv.inc(v.tensor_tensor(ln.mv[:, :], ln.m[:, :], ln.m[:, :], ALU.mult))
                v.wait_ge(sLNv.h, b_v + 2)
                sLNv.inc(v.scalar_tensor_tensor(ln.mv[:, :], P[1][0:1, :], 1.0 / D,
                                                ln.mv[:, :], ALU.mult, ALU.subtract))
            @blk.scalar
            def _(s):
                s.wait_ge(sLNv.h, b_v + 3)
                sLNs.inc(s.activation(ln.lv[:, :], ln.mv[:, :], AF.Ln, bias=eps1[0:1, 0:1]))
                s.wait_ge(sLNs.h, b_s + 1)
                sLNs.inc(s.activation(ln.rstd[:, :], ln.lv[:, :], AF.Exp, scale=-0.5))
        with nc.Block() as blk:
            @blk.tensor
            def _(tns):
                tns.matmul(P[2][:, :], onesrow[:, :], ln.m[:, :], start=True, stop=True)
                tns.matmul(P[3][:, :], onesrow[:, :], ln.rstd[:, :], start=True, stop=True)
        b_v2 = sLNv.n
        b_x = [gdx[0].n, gdx[1].n]
        with nc.Block() as blk:
            if resid_dram is not None:
                xd, xt0 = resid_dram
                @blk.gpsimd
                def _(g):
                    for r in range(NT):
                        if r >= 2:
                            g.wait_ge(sLNv.h, b_v2 + 3 * (r - 1))
                        srcv = xd.ap().rearrange("(t p) f -> p t f", p=128)[
                            :, r:r + 1, xt0:xt0 + CH]
                        gdx[r % 2].dma(g.dma_start(
                            r3(xres2[:, (r % 2) * CH:(r % 2) * CH + CH], 1, CH), srcv))
            @blk.vector
            def _(v):
                nv = 3 if resid_dram is not None else 2
                for r in range(NT):
                    if r >= 1:
                        v.wait_ge(sLNv.h, b_v2 + nv * r)
                    sLNv.inc(v.tensor_tensor(ln.t1[:, :], x_sb[:, r * CH:(r + 1) * CH],
                                             P[2][:, :], ALU.subtract))
                    v.wait_ge(sLNv.h, b_v2 + nv * r + 1)
                    sLNv.inc(v.tensor_tensor(out_sb[:, r * CH:(r + 1) * CH], ln.t1[:, :],
                                             P[3][:, :], ALU.mult))
                    if resid_dram is not None:
                        v.wait_ge(gdx[r % 2].h, b_x[r % 2] + 16 * (r // 2 + 1))
                        v.wait_ge(sLNv.h, b_v2 + nv * r + 2)
                        sLNv.inc(v.tensor_tensor(out2_sb[:, r * CH:(r + 1) * CH],
                                                 out_sb[:, r * CH:(r + 1) * CH],
                                                 xres2[:, (r % 2) * CH:(r % 2) * CH + CH],
                                                 ALU.add))

    import os
    QKMODE = os.environ.get("QKMODE", "full")

    def emit_qk_gen(src_sb, col_base, t0, dest_fn):
        if QKMODE == "plain":
            emit_qk_plain(src_sb, col_base, dest_fn)
            return
        with nc.Block() as blk:
            @blk.gpsimd
            def _(g):
                gd.dma(g.dma_start(cosT[:, :], cos_d.ap()[:, t0:t0 + CH]))
                gd.dma(g.dma_start(sinT[:, :], sin_d.ap()[:, t0:t0 + CH]))
            @blk.sync
            def _(s):
                s.wait_ge(gd.h, gd.n)
        b_A, b_E, b_R, b_P = sA.n, sE.n, sR.n, sP.n
        b_w = [gdw[0].n, gdw[1].n]
        with nc.Block() as blk:
            @blk.gpsimd
            def _(g):
                for j in range(NH):
                    if j >= 2:
                        g.wait_ge(sA.h, b_A + j - 1)
                    buf3 = r3(wcol[:, (j % 2) * 2048:(j % 2) * 2048 + 2048], NT, 128)
                    gdw[j % 2].dma(g.dma_start(buf3, col_view(wqkvT_d, col_base + j * 128, 128)))
            @blk.tensor
            def _(tns):
                for j in range(NH):
                    buf = wcol[:, (j % 2) * 2048:(j % 2) * 2048 + 2048]
                    tns.wait_ge(gdw[j % 2].h, b_w[j % 2] + 16 * (j // 2 + 1))
                    if j >= 1:
                        tns.wait_ge(sE.h, b_E + j)
                    for r in range(NT):
                        mm = tns.matmul(P[4][:, :], buf[:, r * 128:(r + 1) * 128],
                                        src_sb[:, r * CH:(r + 1) * CH],
                                        start=(r == 0), stop=(r == NT - 1))
                    sA.inc(mm)
                    tns.wait_ge(sE.h, b_E + j + 1)
                    if j >= 1:
                        tns.wait_ge(sP.h, b_P + 3 * j - 1)
                    sR.inc(tns.matmul(P[5][:, :], rotm[:, :],
                                      kraw[:, (j % 2) * CH:(j % 2) * CH + CH],
                                      start=True, stop=True))
            @blk.scalar
            def _(s):
                for j in range(NH):
                    s.wait_ge(sA.h, b_A + j + 1)
                    if j >= 2:
                        s.wait_ge(sP.h, b_P + 3 * (j - 2) + 1)
                    sE.inc(s.copy(kraw[:, (j % 2) * CH:(j % 2) * CH + CH], P[4][:, :]))
            @blk.vector
            def _(v):
                for j in range(NH):
                    v.wait_ge(sE.h, b_E + j + 1)
                    if j >= 1:
                        v.wait_ge(sP.h, b_P + 3 * j)
                    sP.inc(v.tensor_tensor(ta[:, :], kraw[:, (j % 2) * CH:(j % 2) * CH + CH],
                                           cosT[:, :], ALU.mult))
                    v.wait_ge(sR.h, b_R + j + 1)
                    sP.inc(v.tensor_tensor(tb[:, :], P[5][:, :], sinT[:, :], ALU.mult))
                    v.wait_ge(sP.h, b_P + 3 * j + 2)
                    sP.inc(v.tensor_tensor(dest_fn(j), ta[:, :], tb[:, :], ALU.add))

    def emit_qk_plain(src_sb, col_base, dest_fn):
        b_A, b_E = sA.n, sE.n
        b_w = [gdw[0].n, gdw[1].n]
        with nc.Block() as blk:
            @blk.gpsimd
            def _(g):
                for j in range(NH):
                    if j >= 2:
                        g.wait_ge(sA.h, b_A + j - 1)
                    buf3 = r3(wcol[:, (j % 2) * 2048:(j % 2) * 2048 + 2048], NT, 128)
                    gdw[j % 2].dma(g.dma_start(buf3, col_view(wqkvT_d, col_base + j * 128, 128)))
            @blk.tensor
            def _(tns):
                for j in range(NH):
                    buf = wcol[:, (j % 2) * 2048:(j % 2) * 2048 + 2048]
                    tns.wait_ge(gdw[j % 2].h, b_w[j % 2] + 16 * (j // 2 + 1))
                    if j >= 1:
                        tns.wait_ge(sE.h, b_E + j)
                    for r in range(NT):
                        mm = tns.matmul(P[4][:, :], buf[:, r * 128:(r + 1) * 128],
                                        src_sb[:, r * CH:(r + 1) * CH],
                                        start=(r == 0), stop=(r == NT - 1))
                    sA.inc(mm)
            @blk.scalar
            def _(s):
                for j in range(NH):
                    s.wait_ge(sA.h, b_A + j + 1)
                    sE.inc(s.copy(dest_fn(j), P[4][:, :]))

    with nc.Block() as blk:
        @blk.gpsimd
        def _(g):
            gd.dma(g.dma_start(rotm[:, :], rot_d.ap()))
            gd.dma(g.dma_start(maskb[:, :], msk_d.ap()))
            gd.dma(g.dma_start(identb[:, :], idn_d.ap()))
            g.memset(ones128[:, :], 1.0)
            g.memset(onesrow[:, :], 1.0)
            g.memset(eps1[:, :], EPS)
        @blk.sync
        def _(s):
            s.wait_ge(gd.h, gd.n)

    for c in range(NCH):
        t0 = c * CH
        with nc.Block() as blk:
            @blk.gpsimd
            def _(g):
                src = x_d.ap().rearrange("(t p) f -> p t f", p=128)[:, 0:NT, t0:t0 + CH]
                gd.dma(g.dma_start(r3(xbf[:, :], NT, CH), src))
            @blk.sync
            def _(s):
                s.wait_ge(gd.h, gd.n)
                s.wait_ge(sd.h, sd.n)

        if "L" in phases:
            emit_ln(xbf, xln)

        if "S" in phases:
            with nc.Block() as blk:
                @blk.sync
                def _(s):
                    for r in range(NT):
                        dst = xln1_d.ap()[r * 128:(r + 1) * 128, t0:t0 + CH]
                        sd.dma(s.dma_start(dst, xln[:, r * CH:(r + 1) * CH]))

        if os.environ.get("DBGOUT") == "probe" and c == 0:
            with nc.Block() as blk:
                @blk.sync
                def _(s):
                    s.wait_ge(sd.h, sd.n)
                    srcv = xln1_d.ap().rearrange("(t p) f -> p t f", p=128)[:, 0:NT, 0:CH]
                    sd.dma(s.dma_start(r3(qch[:, :], NT, CH), srcv))
                    s.wait_ge(sd.h, sd.n)
                    dst = out_d.ap().rearrange("(t p) f -> p t f", p=128)[:, 0:NT, 0:CH]
                    sd.dma(s.dma_start(dst, r3(qch[:, :], NT, CH)))
                    s.wait_ge(sd.h, sd.n)

        if "K" in phases:
            emit_qk_gen(xln, D, t0, lambda j: K[:, j * T + t0:j * T + t0 + CH])

        for kap in range(4 if "V" in phases else 0):
            b_gd, b_A, b_E = gd.n, sA.n, sE.n
            with nc.Block() as blk:
                @blk.gpsimd
                def _(g):
                    gd.dma(g.dma_start(r3(wvbuf[:, :], NT, CH),
                                       col_view(wqkvT_d, 2 * D + kap * CH, CH)))
                @blk.tensor
                def _(tns):
                    tns.wait_ge(gd.h, b_gd + 16)
                    for tau in range(4):
                        if tau >= 1:
                            tns.wait_ge(sE.h, b_E + tau)
                        for r in range(NT):
                            mm = tns.matmul(
                                P[4][:, :],
                                xln[:, r * CH + tau * 128: r * CH + tau * 128 + 128],
                                wvbuf[:, r * CH:(r + 1) * CH],
                                start=(r == 0), stop=(r == NT - 1))
                        sA.inc(mm)
                @blk.scalar
                def _(s):
                    for tau in range(4):
                        s.wait_ge(sA.h, b_A + tau + 1)
                        g_t = 4 * c + tau
                        sE.inc(s.copy(V[:, g_t * D + kap * CH: g_t * D + kap * CH + CH],
                                      P[4][:, :]))

    if "B" not in phases:
        with nc.Block() as blk:
            @blk.sync
            def _(s):
                dst = out_d.ap().rearrange("(t p) f -> p t f", p=128)
                sd.dma(s.dma_start(dst[:, 0:NT, :], r3(K[:, 0:NT * T // 16 * 16][:, 0:NT * 2048 // 16 * 16] if False else K[:, 0:NT * 2048], NT, 2048)))
                s.wait_ge(sd.h, sd.n)
        return nc

    bankE = [0, 0, 0, 0]
    bankM = [0, 0, 0, 0]
    slotT = [0, 0]
    slotM = [0, 0]
    ao_free = 0
    l_free = 0
    b7_free = 0
    ptm_free = 0

    for I in range(NCH):
        t0 = I * CH
        with nc.Block() as blk:
            @blk.sync
            def _(s):
                s.wait_ge(sd.h, sd.n)
                for r in range(NT):
                    src = xln1_d.ap()[r * 128:(r + 1) * 128, t0:t0 + CH]
                    sd.dma(s.dma_start(xln[:, r * CH:(r + 1) * CH], src))
        with nc.Block() as blk:
            @blk.sync
            def _(s):
                s.wait_ge(sd.h, sd.n)

        if os.environ.get("DBGOUT") == "probe" and I == 0:
            with nc.Block() as blk:
                @blk.sync
                def _(s):
                    dst = out_d.ap().rearrange("(t p) f -> p t f", p=128)[:, 0:NT, CH:2 * CH]
                    sd.dma(s.dma_start(dst, r3(xln[:, :], NT, CH)))
                    s.wait_ge(sd.h, sd.n)
            return nc

        if os.environ.get("DBGOUT") == "xln" and I == 0:
            with nc.Block() as blk:
                @blk.sync
                def _(s):
                    dst = out_d.ap().rearrange("(t p) f -> p t f", p=128)[:, 0:NT, 0:CH]
                    sd.dma(s.dma_start(dst, r3(xln[:, :], NT, CH)))
                    s.wait_ge(sd.h, sd.n)
            return nc

        emit_qk_gen(xln, 0, t0, lambda j: qch[:, j * CH:(j + 1) * CH])

        if os.environ.get("DBGOUT") == "qch" and I == 0:
            with nc.Block() as blk:
                @blk.sync
                def _(s):
                    dst = out_d.ap().rearrange("(t p) f -> p t f", p=128)[:, 0:NT, 0:CH]
                    sd.dma(s.dma_start(dst, r3(qch[:, :], NT, CH)))
                    s.wait_ge(sd.h, sd.n)
            return nc

        with nc.Block() as blk:
            plans = []
            for h in range(NH):
                for qt in range(4):
                    QT = 4 * I + qt
                    nkc = QT // 4 + 1
                    widths = [min(CH, 128 * (QT + 1) - kc * CH) for kc in range(nkc)]
                    plans.append((h, qt, QT, nkc, widths))

            it_info = []
            cS, cMX, cEX, cMK = sS.n, sMX.n, sEX.n, sMK.n
            cTP, cPC, cLA = sTP.n, sPC.n, sLA.n
            cEPv, cEPs, cEPt = sEPv.n, sEPs.n, sEPt.n
            _bankE = list(bankE); _bankM = list(bankM)
            _slotT = list(slotT); _slotM = list(slotM)
            _ao, _lf, _b7, _ptm = ao_free, l_free, b7_free, ptm_free
            for (h, qt, QT, nkc, widths) in plans:
                info = {}
                info["bank_free"] = [(_bankE[kc], _bankM[kc]) for kc in range(nkc)]
                info.update(b_S=cS, b_MX=cMX, b_EX=cEX, b_MK=cMK, b_TP=cTP,
                            b_PC=cPC, b_LA=cLA, b_EPv=cEPv, b_EPs=cEPs, b_EPt=cEPt,
                            ao_free=_ao, l_free=_lf, b7_free=_b7, ptm_free=_ptm)
                nt_tiles = sum(w // 128 for w in widths)
                info["slot_free"] = []
                kt_idx = 0
                for kc in range(nkc):
                    info["slot_free"].append((_slotT[kc % 2], _slotM[kc % 2]))
                    _slotT[kc % 2] = cTP + kt_idx + widths[kc] // 128
                    kt_idx += widths[kc] // 128
                    if kc == nkc - 1:
                        _slotM[kc % 2] = cMK + 1
                    _bankE[kc] = cEX + kc + 1
                    _bankM[kc] = cMX + kc + 1
                cS += nkc
                cMX += nkc + 2
                cEX += nkc
                cMK += 1
                cTP += nt_tiles
                cPC += nt_tiles
                cLA += nt_tiles
                cEPv += 2
                cEPs += 1
                cEPt += 1
                _ao = cEPv
                _lf = cEPv - 1
                _b7 = cEPs
                _ptm = cTP
                it_info.append(info)
            bankE, bankM = _bankE, _bankM
            slotT, slotM = _slotT, _slotM
            ao_free, l_free, b7_free, ptm_free = _ao, _lf, _b7, _ptm

            @blk.tensor
            def _(tns):
                for pi, (h, qt, QT, nkc, widths) in enumerate(plans):
                    info = it_info[pi]
                    nt_tiles = sum(w // 128 for w in widths)
                    for kc in range(nkc):
                        w = widths[kc]
                        fE, fM = info["bank_free"][kc]
                        tns.wait_ge(sEX.h, fE)
                        tns.wait_ge(sMX.h, fM)
                        sS.inc(tns.matmul(
                            P[kc][:, 0:w],
                            qch[:, h * CH + qt * 128: h * CH + qt * 128 + 128],
                            K[:, h * T + kc * CH: h * T + kc * CH + w],
                            start=True, stop=True, skip_group_check=True))
                    kt_glob = 0
                    first = True
                    for kc in range(nkc):
                        w = widths[kc]
                        diag = (kc == nkc - 1)
                        for kt in range(w // 128):
                            if diag:
                                tns.wait_ge(sMK.h, info["b_MK"] + 1)
                            else:
                                tns.wait_ge(sEX.h, info["b_EX"] + kc + 1)
                            pin = pexp[:, (kc % 2) * CH + kt * 128:
                                       (kc % 2) * CH + (kt + 1) * 128]
                            tns.wait_ge(sPC.h, info["b_PC"] + kt_glob)
                            sTP.inc(tns.transpose(P[6][:, 0:64].bitcast(BF16),
                                                  pin, identb[:, :]))
                            tns.wait_ge(sPC.h, info["b_PC"] + kt_glob + 1)
                            slot = ptr[:, (kt_glob % 2) * 128:(kt_glob % 2) * 128 + 128]
                            if first:
                                tns.wait_ge(sEPv.h, info["l_free"])
                                tns.wait_ge(sEPv.h, info["ao_free"])
                            kgk = kc * 4 + kt
                            last = (kt_glob == nt_tiles - 1)
                            tns.matmul(P[5][0:1, 0:128], ones128[:, :], slot,
                                       start=first, stop=last, skip_group_check=True)
                            sLA.inc(tns.matmul(
                                P[4][:, 0:128],
                                V[:, kgk * D + h * HD: kgk * D + h * HD + HD],
                                slot, start=first, stop=last, skip_group_check=True))
                            first = False
                            kt_glob += 1
                    tns.wait_ge(sEPv.h, info["b_EPv"] + 1)
                    tns.wait_ge(sEPs.h, info["b7_free"])
                    sEPt.inc(tns.matmul(P[7][:, 0:128], onesrow[:, :],
                                        linv_row[0:1, 0:128], start=True, stop=True,
                                        skip_group_check=True))

            @blk.vector
            def _(v):
                for pi, (h, qt, QT, nkc, widths) in enumerate(plans):
                    info = it_info[pi]
                    for kc in range(nkc):
                        w = widths[kc]
                        v.wait_ge(sS.h, info["b_S"] + kc + 1)
                        v.wait_ge(sMX.h, info["b_MX"])
                        sMX.inc(v.reduce_max(mxp[:, kc:kc + 1], P[kc][:, 0:w], axis=AXX))
                    v.wait_ge(sMX.h, info["b_MX"] + nkc)
                    sMX.inc(v.reduce_max(mrow[:, :], mxp[:, 0:nkc], axis=AXX))
                    v.wait_ge(sMX.h, info["b_MX"] + nkc + 1)
                    v.wait_ge(sEX.h, info["b_EX"])
                    sMX.inc(v.tensor_scalar_mul(negm[:, :], mrow[:, :], -SCALE))
                    kc = nkc - 1
                    w = widths[kc]
                    vvar = QT % 4
                    v.wait_ge(sEX.h, info["b_EX"] + kc + 1)
                    sMK.inc(v.tensor_tensor(pexp[:, (kc % 2) * CH:(kc % 2) * CH + w],
                                            pexp[:, (kc % 2) * CH:(kc % 2) * CH + w],
                                            maskb[:, vvar * CH:vvar * CH + w], ALU.mult))
                    v.wait_ge(sLA.h, info["b_LA"] + sum(ww // 128 for ww in widths))
                    sEPv.inc(v.reciprocal(linv_row[0:1, 0:128], P[5][0:1, 0:128]))
                    v.wait_ge(sEPs.h, info["b_EPs"] + 1)
                    sEPv.inc(v.tensor_tensor(
                        att[:, h * CH + qt * 128: h * CH + qt * 128 + 128],
                        P[4][:, 0:128], linv_rep[:, 0:128], ALU.mult))

            @blk.scalar
            def _(s):
                for pi, (h, qt, QT, nkc, widths) in enumerate(plans):
                    info = it_info[pi]

                    def emit_copies(kc_c, kt_base, s=s, info=info):
                        kt_g = kt_base
                        for kt in range(widths[kc_c] // 128):
                            s.wait_ge(sTP.h, info["b_TP"] + kt_g + 1)
                            if kt_g >= 2:
                                s.wait_ge(sLA.h, info["b_LA"] + kt_g - 1)
                            sPC.inc(s.copy(
                                ptr[:, (kt_g % 2) * 128:(kt_g % 2) * 128 + 128],
                                P[6][:, 0:64].bitcast(BF16)))
                            kt_g += 1
                        return kt_g

                    kt_done = 0
                    kt_bases = []
                    kb = 0
                    for kc in range(nkc):
                        kt_bases.append(kb)
                        kb += widths[kc] // 128
                    for kc in range(nkc):
                        w = widths[kc]
                        s.wait_ge(sMX.h, info["b_MX"] + nkc + 2)
                        fT, fM2 = info["slot_free"][kc]
                        s.wait_ge(sTP.h, fT)
                        s.wait_ge(sMK.h, fM2)
                        sEX.inc(s.activation(pexp[:, (kc % 2) * CH:(kc % 2) * CH + w],
                                             P[kc][:, 0:w], AF.Exp,
                                             bias=negm[:, :], scale=SCALE))
                        if kc >= 1:
                            emit_copies(kc - 1, kt_bases[kc - 1])
                    emit_copies(nkc - 1, kt_bases[nkc - 1])
                    s.wait_ge(sEPt.h, info["b_EPt"] + 1)
                    sEPs.inc(s.copy(linv_rep[:, 0:128], P[7][:, 0:128]))

        if os.environ.get("DBGOUT") == "att" and I == 0:
            with nc.Block() as blk:
                @blk.sync
                def _(s):
                    dst = out_d.ap().rearrange("(t p) f -> p t f", p=128)[:, 0:NT, 0:CH]
                    sd.dma(s.dma_start(dst, r3(att[:, :], NT, CH)))
                    s.wait_ge(sd.h, sd.n)
            return nc

        b_A, b_E = sA.n, sE.n
        b_w = [gdw[0].n, gdw[1].n]
        with nc.Block() as blk:
            @blk.gpsimd
            def _(g):
                for j in range(NT):
                    if j >= 2:
                        g.wait_ge(sA.h, b_A + j - 1)
                    buf3 = r3(wcol[:, (j % 2) * 2048:(j % 2) * 2048 + 2048], NT, 128)
                    gdw[j % 2].dma(g.dma_start(buf3, col_view(woutT_d, j * 128, 128)))
            @blk.tensor
            def _(tns):
                for j in range(NT):
                    buf = wcol[:, (j % 2) * 2048:(j % 2) * 2048 + 2048]
                    tns.wait_ge(gdw[j % 2].h, b_w[j % 2] + 16 * (j // 2 + 1))
                    if j >= 1:
                        tns.wait_ge(sE.h, b_E + j)
                    for r in range(NT):
                        mm = tns.matmul(P[4][:, :], buf[:, r * 128:(r + 1) * 128],
                                        att[:, r * CH:(r + 1) * CH],
                                        start=(r == 0), stop=(r == NT - 1))
                    sA.inc(mm)
            @blk.scalar
            def _(s):
                for j in range(NT):
                    s.wait_ge(sA.h, b_A + j + 1)
                    sE.inc(s.copy(qch[:, j * CH:(j + 1) * CH], P[4][:, :]))

        emit_ln(qch, xln, resid_dram=(x_d, t0), out2_sb=att)

        with nc.Block() as blk:
            @blk.sync
            def _(s):
                dst = out_d.ap().rearrange("(t p) f -> p t f", p=128)[:, 0:NT, t0:t0 + CH]
                sd.dma(s.dma_start(dst, r3(att[:, :], NT, CH)))

    with nc.Block() as blk:
        @blk.sync
        def _(s):
            s.wait_ge(sd.h, sd.n)
    return nc


def build_attn(phases="LSKVB"):
    nc = bass.Bass("TRN2", target_bir_lowering=False, dynamic_dma_scratch_size=16384)
    x_d = nc.dram_tensor("x_t", [D, T], F32, kind="ExternalInput")
    wqkvT_d = nc.dram_tensor("wqkvT", [D, 3 * D], F32, kind="ExternalInput")
    woutT_d = nc.dram_tensor("woutT", [D, D], F32, kind="ExternalInput")
    cos_d = nc.dram_tensor("cosT", [HD, T], F32, kind="ExternalInput")
    sin_d = nc.dram_tensor("sinT", [HD, T], F32, kind="ExternalInput")
    rot_d = nc.dram_tensor("rotm", [HD, HD], F32, kind="ExternalInput")
    msk_d = nc.dram_tensor("maskblk", [128, 4 * CH], F32, kind="ExternalInput")
    idn_d = nc.dram_tensor("identm", [128, 128], F32, kind="ExternalInput")
    out_d = nc.dram_tensor("resid1", [D, T], BF16, kind="ExternalOutput")
    xln1_d = nc.dram_tensor("xln1_scr", [D, T], BF16, kind="ExternalOutput")
    build_attn_body(nc, x_d, wqkvT_d, woutT_d, cos_d, sin_d, rot_d, msk_d,
                    idn_d, out_d, xln1_d, phases=phases)
    return nc


HF = 1024
EH = 1408
NHT = 11
SH = 2816
NST = 22
NE = 16


def moe_host_inputs(resid1_bf16, gate_w, gate_b, w1, b1, w2, b2, w3, b3,
                    ws1, bs1, ws2, bs2, ws3, bs3):
    def packcol(b):
        return np.ascontiguousarray(b.reshape(-1, 128).T.astype(np.float32))
    esel = np.zeros((16, 16 * 128), np.float32)
    for e in range(16):
        esel[e, e * 128:(e + 1) * 128] = 1.0
    return {
        "resid1": resid1_bf16,
        "gate_wT": np.ascontiguousarray(gate_w.T),
        "gbrep": np.ascontiguousarray(np.tile(np.asarray(gate_b)[None, :], (128, 1)).astype(np.float32)),
        "w1T": np.ascontiguousarray(w1.transpose(0, 2, 1)),
        "w3T": np.ascontiguousarray(w3.transpose(0, 2, 1)),
        "w2T": np.ascontiguousarray(w2.transpose(0, 2, 1)),
        "b1t": packcol(b1), "b3t": packcol(b3),
        "b2sb": np.ascontiguousarray(b2),
        "ws1T": np.ascontiguousarray(ws1.T), "ws3T": np.ascontiguousarray(ws3.T),
        "ws2T": np.ascontiguousarray(ws2.T),
        "bs1t": packcol(bs1), "bs3t": packcol(bs3), "bs2t": packcol(bs2),
        "identm": np.eye(128, dtype=np.float32),
        "eselm": esel,
    }


def build_moe_body(nc, rin_d, gate_wT_d, gbrep_d, w1T_d, w3T_d, w2T_d, b1t_d,
                   b3t_d, b2_d, ws1T_d, ws3T_d, ws2T_d, bs1t_d, bs3t_d,
                   bs2t_d, idn_d, esel_d, out_d):
    ctx = nc.ctx
    gd = Sem(nc, "gd")
    sd = Sem(nc, "sd")
    sLNsq = Sem(nc, "sLNsq"); sLNq = Sem(nc, "sLNq")
    sLNv = Sem(nc, "sLNv"); sLNs = Sem(nc, "sLNs")
    gdx = [Sem(nc, "gdx0"), Sem(nc, "gdx1")]
    gw13 = [Sem(nc, "gw13_0"), Sem(nc, "gw13_1")]
    gw2 = [Sem(nc, "gw2_0"), Sem(nc, "gw2_1")]
    sZ = Sem(nc, "sZ"); sSi = Sem(nc, "sSi"); sH = Sem(nc, "sH")
    sCW = Sem(nc, "sCW"); sY = Sem(nc, "sY"); sYA = Sem(nc, "sYA")
    sG = Sem(nc, "sG")

    xln = ctx.enter_context(nc.sbuf_tensor("xlnm", [128, NT * HF], BF16))
    y = ctx.enter_context(nc.sbuf_tensor("ymoe", [128, NT * HF], F32))
    h2 = ctx.enter_context(nc.sbuf_tensor("h2", [128, NHT * HF], BF16))
    sA = ctx.enter_context(nc.sbuf_tensor("sAbuf", [128, CH], F32))
    w1c = ctx.enter_context(nc.sbuf_tensor("w1c", [128, 2 * 2048], BF16))
    w3c = ctx.enter_context(nc.sbuf_tensor("w3c", [128, 2 * 2048], BF16))
    w2c = ctx.enter_context(nc.sbuf_tensor("w2c", [128, 2 * NHT * 128], BF16))
    rbf = ctx.enter_context(nc.sbuf_tensor("rbf", [128, NT * CH], BF16))
    ybf = ctx.enter_context(nc.sbuf_tensor("ybf", [128, CH], BF16))
    xres2 = ctx.enter_context(nc.sbuf_tensor("xres2m", [128, 2 * CH], BF16))
    sqb = ctx.enter_context(nc.sbuf_tensor("sqbm", [128, 2 * CH], BF16))
    ones128 = ctx.enter_context(nc.sbuf_tensor("ones128m", [128, 1], BF16))
    onesrow = ctx.enter_context(nc.sbuf_tensor("onesrowm", [1, 128], F32))
    eps1 = ctx.enter_context(nc.sbuf_tensor("eps1m", [1, 1], F32))
    identf = ctx.enter_context(nc.sbuf_tensor("identfm", [128, 128], F32))
    eselb = ctx.enter_context(nc.sbuf_tensor("eselb", [16, 16 * 128], BF16))
    gwsb = ctx.enter_context(nc.sbuf_tensor("gwsb", [128, NT * 16], BF16))
    gbrep = ctx.enter_context(nc.sbuf_tensor("gbrep_sb", [128, 16], F32))
    scsb = ctx.enter_context(nc.sbuf_tensor("scsb", [16, CH], F32))
    toksc = ctx.enter_context(nc.sbuf_tensor("toksc", [128, 16], F32))
    probs = ctx.enter_context(nc.sbuf_tensor("probs", [128, 16], F32))
    sel = ctx.enter_context(nc.sbuf_tensor("selb", [128, 16], F32))
    cur = ctx.enter_context(nc.sbuf_tensor("curb", [128, 2 * 16], F32))
    geb = ctx.enter_context(nc.sbuf_tensor("geb", [128, 16], F32))
    negt = ctx.enter_context(nc.sbuf_tensor("negt", [128, 16], F32))
    nmx = ctx.enter_context(nc.sbuf_tensor("nmx", [128, 1], F32))
    srow = ctx.enter_context(nc.sbuf_tensor("srow", [128, 1], F32))
    cwT = ctx.enter_context(nc.sbuf_tensor("cwT", [16, 2 * CH], BF16))
    cwrep = ctx.enter_context(nc.sbuf_tensor("cwrep", [128, 2 * CH], BF16))
    b1t = ctx.enter_context(nc.sbuf_tensor("b1t_sb", [128, NE * NHT], F32))
    b3t = ctx.enter_context(nc.sbuf_tensor("b3t_sb", [128, NE * NHT], F32))
    b2sb = ctx.enter_context(nc.sbuf_tensor("b2_sb", [16, D], BF16))
    bs1t = ctx.enter_context(nc.sbuf_tensor("bs1t_sb", [128, NST], F32))
    bs3t = ctx.enter_context(nc.sbuf_tensor("bs3t_sb", [128, NST], F32))
    bs2t = ctx.enter_context(nc.sbuf_tensor("bs2t_sb", [128, NT], F32))

    class _L: pass
    ln = _L()
    ln.m = ctx.enter_context(nc.sbuf_tensor("ln_m", [1, CH], F32))
    ln.mv = ctx.enter_context(nc.sbuf_tensor("ln_mv", [1, CH], F32))
    ln.lv = ln.mv; ln.rstd = ln.mv
    ln.t1 = ctx.enter_context(nc.sbuf_tensor("ln_t1", [128, CH], F32))

    P = [ctx.enter_context(nc.psum_tensor(f"P{i}", [128, CH], F32))
         for i in range(8)]

    def emit_ln2(x_get, out_get, xcast_needed=False, resid_dram=None,
                 out2_get=None):
        b_sq, b_q, b_c = sLNsq.n, sLNq.n, sLNv.n
        b_sd0 = sd.n
        with nc.Block() as blk:
            if xcast_needed:
                @blk.vector
                def _(v):
                    v.wait_ge(sd.h, b_sd0)
                    for r in range(NT):
                        sLNv.inc(v.tensor_copy(rbf[:, r * CH:(r + 1) * CH], x_get(r)))
            @blk.scalar
            def _(s):
                for r in range(NT):
                    buf = sqb[:, (r % 2) * CH:(r % 2) * CH + CH]
                    if r >= 2:
                        s.wait_ge(sLNq.h, b_q + r - 1)
                    sLNsq.inc(s.activation(buf, x_get(r), AF.Square))
            @blk.tensor
            def _(tns):
                for r in range(NT):
                    if xcast_needed:
                        tns.wait_ge(sLNv.h, b_c + r + 1)
                        xs = rbf[:, r * CH:(r + 1) * CH]
                    else:
                        xs = x_get(r)
                    tns.matmul(P[0][0:1, :], ones128[:, :], xs,
                               start=(r == 0), stop=(r == NT - 1))
                for r in range(NT):
                    buf = sqb[:, (r % 2) * CH:(r % 2) * CH + CH]
                    tns.wait_ge(sLNsq.h, b_sq + r + 1)
                    sLNq.inc(tns.matmul(P[1][0:1, :], ones128[:, :], buf,
                                        start=(r == 0), stop=(r == NT - 1)))
        b_v, b_s = sLNv.n, sLNs.n
        with nc.Block() as blk:
            @blk.vector
            def _(v):
                sLNv.inc(v.tensor_scalar_mul(ln.m[:, :], P[0][0:1, :], 1.0 / D))
                v.wait_ge(sLNv.h, b_v + 1)
                sLNv.inc(v.tensor_tensor(ln.mv[:, :], ln.m[:, :], ln.m[:, :], ALU.mult))
                v.wait_ge(sLNv.h, b_v + 2)
                sLNv.inc(v.scalar_tensor_tensor(ln.mv[:, :], P[1][0:1, :], 1.0 / D,
                                                ln.mv[:, :], ALU.mult, ALU.subtract))
            @blk.scalar
            def _(s):
                s.wait_ge(sLNv.h, b_v + 3)
                sLNs.inc(s.activation(ln.lv[:, :], ln.mv[:, :], AF.Ln, bias=eps1[0:1, 0:1]))
                s.wait_ge(sLNs.h, b_s + 1)
                sLNs.inc(s.activation(ln.rstd[:, :], ln.lv[:, :], AF.Exp, scale=-0.5))
        with nc.Block() as blk:
            @blk.tensor
            def _(tns):
                tns.matmul(P[2][:, :], onesrow[:, :], ln.m[:, :], start=True, stop=True)
                tns.matmul(P[3][:, :], onesrow[:, :], ln.rstd[:, :], start=True, stop=True)
        b_v2 = sLNv.n
        b_x = [gdx[0].n, gdx[1].n]
        with nc.Block() as blk:
            if resid_dram is not None:
                xd, xt0 = resid_dram
                @blk.gpsimd
                def _(g):
                    for r in range(NT):
                        if r >= 2:
                            g.wait_ge(sLNv.h, b_v2 + 3 * (r - 1))
                        srcv = xd.ap()[r * 128:(r + 1) * 128, xt0:xt0 + CH]
                        gdx[r % 2].dma(g.dma_start(
                            xres2[:, (r % 2) * CH:(r % 2) * CH + CH], srcv))
            @blk.vector
            def _(v):
                nv = 3 if resid_dram is not None else 2
                for r in range(NT):
                    if r >= 1:
                        v.wait_ge(sLNv.h, b_v2 + nv * r)
                    sLNv.inc(v.tensor_tensor(ln.t1[:, :], x_get(r), P[2][:, :],
                                             ALU.subtract))
                    v.wait_ge(sLNv.h, b_v2 + nv * r + 1)
                    sLNv.inc(v.tensor_tensor(out_get(r), ln.t1[:, :], P[3][:, :],
                                             ALU.mult))
                    if resid_dram is not None:
                        v.wait_ge(gdx[r % 2].h, b_x[r % 2] + 16 * (r // 2 + 1))
                        v.wait_ge(sLNv.h, b_v2 + nv * r + 2)
                        sLNv.inc(v.tensor_tensor(out2_get(r), out_get(r),
                                                 xres2[:, (r % 2) * CH:(r % 2) * CH + CH],
                                                 ALU.add))

    with nc.Block() as blk:
        @blk.gpsimd
        def _(g):
            gd.dma(g.dma_start(gwsb.rearrange("p (t f) -> p t f", t=NT),
                               gate_wT_d.ap().rearrange("(t p) f -> p t f", p=128)))
            gd.dma(g.dma_start(gbrep[:, :], gbrep_d.ap()))
            gd.dma(g.dma_start(b1t[:, :], b1t_d.ap()))
            gd.dma(g.dma_start(b3t[:, :], b3t_d.ap()))
            gd.dma(g.dma_start(b2sb[:, :], b2_d.ap()))
            gd.dma(g.dma_start(bs1t[:, :], bs1t_d.ap()))
            gd.dma(g.dma_start(bs3t[:, :], bs3t_d.ap()))
            gd.dma(g.dma_start(bs2t[:, :], bs2t_d.ap()))
            gd.dma(g.dma_start(identf[:, :], idn_d.ap()))
            gd.dma(g.dma_start(eselb[:, :], esel_d.ap()))
            g.memset(ones128[:, :], 1.0)
            g.memset(onesrow[:, :], 1.0)
            g.memset(eps1[:, :], EPS)
            g.memset(negt[:, :], -1e30)
        @blk.sync
        def _(s):
            s.wait_ge(gd.h, gd.n)

    wt13 = {}
    wt2 = {}

    def mk13(view1, view3, slot, key):
        def em(g):
            gw13[slot].dma(g.dma_start(
                r3(w1c[:, slot * 2048:slot * 2048 + 2048], NT, 128), view1))
            gw13[slot].dma(g.dma_start(
                r3(w3c[:, slot * 2048:slot * 2048 + 2048], NT, 128), view2_ := view3))
            wt13[key] = gw13[slot].n
        return em

    def mk2(view, ntl, slot2, key):
        def em(g):
            gw2[slot2].dma(g.dma_start(
                r3(w2c[:, slot2 * NHT * 128:slot2 * NHT * 128 + ntl * 128], ntl, 128),
                view))
            wt2[key] = gw2[slot2].n
        return em

    from contextlib import nullcontext

    def emit_z_htile(b1col, b3col, h2col, slot, key, next_em, blk=None):
        b_z, b_si, b_h = sZ.n, sSi.n, sH.n
        with (nc.Block() if blk is None else nullcontext(blk)) as blk:
            @blk.gpsimd
            def _(g):
                g.wait_ge(sZ.h, b_z)
                if next_em is not None:
                    next_em(g)
            @blk.tensor
            def _(tns):
                tns.wait_ge(gw13[slot].h, wt13[key])
                tns.wait_ge(sSi.h, b_si)
                tns.wait_ge(sH.h, b_h)
                for cc in range(2):
                    if cc == 1:
                        tns.wait_ge(sH.h, b_h + 1)
                    for r in range(NT):
                        tns.matmul(P[4][:, :],
                                   w1c[:, slot * 2048 + r * 128: slot * 2048 + (r + 1) * 128],
                                   xln[:, r * HF + cc * CH: r * HF + cc * CH + CH],
                                   start=(r == 0), stop=(r == NT - 1))
                    for r in range(NT):
                        mm = tns.matmul(P[5][:, :],
                                        w3c[:, slot * 2048 + r * 128: slot * 2048 + (r + 1) * 128],
                                        xln[:, r * HF + cc * CH: r * HF + cc * CH + CH],
                                        start=(r == 0), stop=(r == NT - 1))
                    sZ.inc(mm)
            @blk.scalar
            def _(s):
                s.wait_ge(sH.h, b_h)
                for cc in range(2):
                    s.wait_ge(sZ.h, b_z + cc + 1)
                    if cc == 1:
                        s.wait_ge(sH.h, b_h + 1)
                    sSi.inc(s.activation(sA[:, :], P[4][:, :], AF.Silu, bias=b1col))
            @blk.vector
            def _(v):
                for cc in range(2):
                    v.wait_ge(sSi.h, b_si + cc + 1)
                    sH.inc(v.scalar_tensor_tensor(h2col(cc), P[5][:, :], b3col,
                                                  sA[:, :], ALU.add, ALU.mult))

    def emit_w2_dtile(ntl, dt, slot2, first_group, with_b2, cwT_ref, key, next_em,
                      blk=None):
        b_y, b_ya = sY.n, sYA.n
        with (nc.Block() if blk is None else nullcontext(blk)) as blk:
            @blk.gpsimd
            def _(g):
                g.wait_ge(sY.h, b_y)
                if next_em is not None:
                    next_em(g)
            @blk.tensor
            def _(tns):
                tns.wait_ge(gw2[slot2].h, wt2[key])
                tns.wait_ge(sYA.h, b_ya)
                for cc in range(2):
                    if cc == 1:
                        tns.wait_ge(sYA.h, b_ya + 1)
                    started = False
                    if with_b2:
                        tns.matmul(P[6][:, :], b2sb[:, dt * 128:(dt + 1) * 128],
                                   cwT_ref[:, cc * CH:(cc + 1) * CH],
                                   start=True, stop=False, skip_group_check=True)
                        started = True
                    for hh in range(ntl):
                        mm = tns.matmul(
                            P[6][:, :],
                            w2c[:, slot2 * NHT * 128 + hh * 128: slot2 * NHT * 128 + (hh + 1) * 128],
                            h2[:, hh * HF + cc * CH: hh * HF + cc * CH + CH],
                            start=(not started and hh == 0), stop=(hh == ntl - 1),
                            skip_group_check=True)
                    sY.inc(mm)
            @blk.vector
            def _(v):
                for cc in range(2):
                    v.wait_ge(sY.h, b_y + cc + 1)
                    ysl = y[:, dt * HF + cc * CH: dt * HF + cc * CH + CH]
                    if first_group:
                        sYA.inc(v.tensor_scalar(ysl, P[6][:, :], bs2t[:, dt:dt + 1],
                                                None, ALU.add))
                    else:
                        sYA.inc(v.tensor_tensor(ysl, ysl, P[6][:, :], ALU.add))

    for half in range(2):
        th0 = half * HF
        for cc0 in range(2):
            t0 = th0 + cc0 * CH
            with nc.Block() as blk:
                @blk.gpsimd
                def _(g):
                    g.wait_ge(sd.h, sd.n)
                    src = rin_d.ap().rearrange("(t p) f -> p t f", p=128)[:, 0:NT, t0:t0 + CH]
                    gd.dma(g.dma_start(r3(rbf[:, :], NT, CH), src))
                @blk.sync
                def _(s):
                    s.wait_ge(gd.h, gd.n)
            emit_ln2(lambda r: rbf[:, r * CH:(r + 1) * CH],
                     lambda r, cc0=cc0: xln[:, r * HF + cc0 * CH: r * HF + cc0 * CH + CH])

        for cc in range(2):
            b_g = sG.n
            with nc.Block() as blk:
                @blk.tensor
                def _(tns):
                    for r in range(NT):
                        mm = tns.matmul(P[7][0:16, :], gwsb[:, r * 16:(r + 1) * 16],
                                        xln[:, r * HF + cc * CH: r * HF + cc * CH + CH],
                                        start=(r == 0), stop=(r == NT - 1))
                    sG.inc(mm)
                @blk.scalar
                def _(s):
                    s.wait_ge(sG.h, b_g + 1)
                    sG.inc(s.copy(scsb[:, :], P[7][0:16, :]))
            for tau in range(4):
                b_g2 = sG.n
                with nc.Block() as blk:
                    @blk.tensor
                    def _(tns):
                        sG.inc(tns.transpose(P[7][:, 0:16],
                                             scsb[:, tau * 128:(tau + 1) * 128],
                                             identf[0:16, 0:16]))
                    @blk.scalar
                    def _(s):
                        s.wait_ge(sG.h, b_g2 + 1)
                        sG.inc(s.copy(toksc[:, :], P[7][:, 0:16]))
                b_g2 = sG.n
                with nc.Block() as blk:
                    @blk.vector
                    def _(v):
                        sG.inc(v.reduce_max(nmx[:, :], toksc[:, :], axis=AXX,
                                            negate=True))
                    @blk.scalar
                    def _(s):
                        s.wait_ge(sG.h, b_g2 + 1)
                        sG.inc(s.activation(probs[:, :], toksc[:, :], AF.Exp,
                                            bias=nmx[:, :], accum_out=srow[:, :]))
                with nc.Block() as blk:
                    @blk.vector
                    def _(v):
                        b = sG.n
                        sG.inc(v.reciprocal(nmx[:, :], srow[:, :]))
                        v.wait_ge(sG.h, b + 1)
                        sG.inc(v.tensor_scalar_mul(probs[:, :], probs[:, :], nmx[:, 0:1]))
                        v.wait_ge(sG.h, b + 2)
                        sG.inc(v.tensor_tensor(cur[:, 0:16], probs[:, :], gbrep[:, :],
                                               ALU.add))
                        v.wait_ge(sG.h, b + 3)
                        sG.inc(v.memset(sel[:, :], 0.0))
                        for it in range(6):
                            ci = (it % 2) * 16
                            co = ((it + 1) % 2) * 16
                            bi = sG.n
                            v.wait_ge(sG.h, bi)
                            sG.inc(v.reduce_max(nmx[:, :], cur[:, ci:ci + 16], axis=AXX))
                            v.wait_ge(sG.h, bi + 1)
                            sG.inc(v.tensor_single_scalar(geb[:, :], cur[:, ci:ci + 16],
                                                          nmx[:, 0:1], ALU.is_ge))
                            v.wait_ge(sG.h, bi + 2)
                            sG.inc(v.tensor_max(sel[:, :], sel[:, :], geb[:, :]))
                            v.wait_ge(sG.h, bi + 3)
                            sG.inc(v.scalar_tensor_tensor(cur[:, co:co + 16], geb[:, :],
                                                          -1e30, cur[:, ci:ci + 16],
                                                          ALU.mult, ALU.add))
                        bi = sG.n
                        v.wait_ge(sG.h, bi)
                        sG.inc(v.tensor_tensor(probs[:, :], probs[:, :], sel[:, :],
                                               ALU.mult))
                b_g3 = sG.n
                with nc.Block() as blk:
                    @blk.tensor
                    def _(tns):
                        sG.inc(tns.transpose(P[7][0:16, 0:128], probs[:, :],
                                             identf[:, :]))
                    @blk.scalar
                    def _(s):
                        s.wait_ge(sG.h, b_g3 + 1)
                        sG.inc(s.copy(cwT[:, cc * CH + tau * 128: cc * CH + (tau + 1) * 128],
                                      P[7][0:16, 0:128]))

        steps = []
        for grp in range(2):
            for hh in range(NHT):
                st = grp * NHT + hh
                k = ("sz", half, grp, hh)
                steps.append(("z", k, mk13(col_view(ws1T_d, st * 128, 128),
                                           col_view(ws3T_d, st * 128, 128), hh % 2, k),
                              bs1t[:, st:st + 1], bs3t[:, st:st + 1], hh, hh % 2,
                              None, None, None))
            for dt in range(NT):
                k = ("sw", half, grp, dt)
                w2view = ws2T_d.ap().rearrange("(t p) f -> p t f", p=128)[
                    :, grp * NHT:grp * NHT + NHT, dt * 128:(dt + 1) * 128]
                steps.append(("w2", k, mk2(w2view, NHT, dt % 2, k), None, None,
                              dt, dt % 2, (grp == 0), False, None))
        for e in range(NE):
            for hh in range(NHT):
                k = ("ez", half, e, hh)
                steps.append(("z", k,
                              mk13(w1T_d.ap().rearrange("e (t p) f -> e p t f", p=128)[
                                       e, :, 0:NT, hh * 128:(hh + 1) * 128],
                                   w3T_d.ap().rearrange("e (t p) f -> e p t f", p=128)[
                                       e, :, 0:NT, hh * 128:(hh + 1) * 128], hh % 2, k),
                              b1t[:, e * NHT + hh:e * NHT + hh + 1],
                              b3t[:, e * NHT + hh:e * NHT + hh + 1], hh, hh % 2,
                              None, None, None))
            for dt in range(NT):
                k = ("ew", half, e, dt)
                w2view = w2T_d.ap().rearrange("e (t p) f -> e p t f", p=128)[
                    e, :, 0:NHT, dt * 128:(dt + 1) * 128]
                steps.append(("w2", k, mk2(w2view, NHT, dt % 2, k), None, None,
                              dt, dt % 2, False, (e == 0), None))

        with nc.Block() as blk:
            @blk.gpsimd
            def _(g):
                steps[0][2](g)

        si = 0
        GRP = 4

        def run_z(n):
            nonlocal si
            j = 0
            while j < n:
                take = min(GRP, n - j)
                with nc.Block() as blk:
                    for _t in range(take):
                        kind, k, em, b1c, b3c, hh2, slot, fg, wb2, _x = steps[si]
                        nxt = steps[si + 1][2] if si + 1 < len(steps) else None
                        emit_z_htile(b1c, b3c,
                                     (lambda cc, hh3=hh2: h2[:, hh3 * HF + cc * CH: hh3 * HF + cc * CH + CH]),
                                     slot, k, nxt, blk=blk)
                        si += 1
                j += take

        def run_w2(n):
            nonlocal si
            j = 0
            while j < n:
                take = min(GRP, n - j)
                with nc.Block() as blk:
                    for _t in range(take):
                        kind, k, em, _a, _b, dt2, slot2, fg, wb2, _x = steps[si]
                        nxt = steps[si + 1][2] if si + 1 < len(steps) else None
                        emit_w2_dtile(NHT, dt2, slot2, first_group=fg, with_b2=wb2,
                                      cwT_ref=cwT, key=k, next_em=nxt, blk=blk)
                        si += 1
                j += take

        for grp in range(2):
            run_z(NHT)
            run_w2(NT)
        for e in range(NE):
            run_z(NHT)

            b_g4 = sG.n
            with nc.Block() as blk:
                @blk.tensor
                def _(tns):
                    sG.inc(tns.matmul(P[7][:, :], eselb[:, e * 128:(e + 1) * 128],
                                      cwT[:, 0:CH], start=True, stop=True,
                                      skip_group_check=True))
                    tns.wait_ge(sG.h, b_g4 + 2)
                    sG.inc(tns.matmul(P[7][:, :], eselb[:, e * 128:(e + 1) * 128],
                                      cwT[:, CH:2 * CH], start=True, stop=True,
                                      skip_group_check=True))
                @blk.scalar
                def _(s):
                    s.wait_ge(sG.h, b_g4 + 1)
                    sG.inc(s.copy(cwrep[:, 0:CH], P[7][:, :]))
                    s.wait_ge(sG.h, b_g4 + 3)
                    sG.inc(s.copy(cwrep[:, CH:2 * CH], P[7][:, :]))
            with nc.Block() as blk:
                @blk.vector
                def _(v):
                    for hh in range(NHT):
                        for cc in range(2):
                            sCW.inc(v.tensor_tensor(
                                h2[:, hh * HF + cc * CH: hh * HF + cc * CH + CH],
                                h2[:, hh * HF + cc * CH: hh * HF + cc * CH + CH],
                                cwrep[:, cc * CH:(cc + 1) * CH], ALU.mult))

            run_w2(NT)

        for cc0 in range(2):
            t0 = th0 + cc0 * CH
            emit_ln2(lambda r, cc0=cc0: y[:, r * HF + cc0 * CH: r * HF + cc0 * CH + CH],
                     lambda r: ybf[:, :],
                     xcast_needed=True,
                     resid_dram=(rin_d, t0),
                     out2_get=lambda r: rbf[:, r * CH:(r + 1) * CH])
            with nc.Block() as blk:
                @blk.sync
                def _(s):
                    dst = out_d.ap().rearrange("(t p) f -> p t f", p=128)[:, 0:NT, t0:t0 + CH]
                    sd.dma(s.dma_start(dst, r3(rbf[:, :], NT, CH)))

    with nc.Block() as blk:
        @blk.sync
        def _(s):
            s.wait_ge(sd.h, sd.n)
    return nc


def build_moe():
    nc = bass.Bass("TRN2", target_bir_lowering=False)
    rin_d = nc.dram_tensor("resid1", [D, T], BF16, kind="ExternalInput")
    gate_wT_d = nc.dram_tensor("gate_wT", [D, 16], F32, kind="ExternalInput")
    gbrep_d = nc.dram_tensor("gbrep", [128, 16], F32, kind="ExternalInput")
    w1T_d = nc.dram_tensor("w1T", [NE, D, EH], F32, kind="ExternalInput")
    w3T_d = nc.dram_tensor("w3T", [NE, D, EH], F32, kind="ExternalInput")
    w2T_d = nc.dram_tensor("w2T", [NE, EH, D], F32, kind="ExternalInput")
    b1t_d = nc.dram_tensor("b1t", [128, NE * NHT], F32, kind="ExternalInput")
    b3t_d = nc.dram_tensor("b3t", [128, NE * NHT], F32, kind="ExternalInput")
    b2_d = nc.dram_tensor("b2sb", [16, D], F32, kind="ExternalInput")
    ws1T_d = nc.dram_tensor("ws1T", [D, SH], F32, kind="ExternalInput")
    ws3T_d = nc.dram_tensor("ws3T", [D, SH], F32, kind="ExternalInput")
    ws2T_d = nc.dram_tensor("ws2T", [SH, D], F32, kind="ExternalInput")
    bs1t_d = nc.dram_tensor("bs1t", [128, NST], F32, kind="ExternalInput")
    bs3t_d = nc.dram_tensor("bs3t", [128, NST], F32, kind="ExternalInput")
    bs2t_d = nc.dram_tensor("bs2t", [128, NT], F32, kind="ExternalInput")
    idn_d = nc.dram_tensor("identm", [128, 128], F32, kind="ExternalInput")
    esel_d = nc.dram_tensor("eselm", [16, 16 * 128], F32, kind="ExternalInput")
    out_d = nc.dram_tensor("resid2", [D, T], BF16, kind="ExternalOutput")
    build_moe_body(nc, rin_d, gate_wT_d, gbrep_d, w1T_d, w3T_d, w2T_d, b1t_d,
                   b3t_d, b2_d, ws1T_d, ws3T_d, ws2T_d, bs1t_d, bs3t_d,
                   bs2t_d, idn_d, esel_d, out_d)
    return nc


NG = 16
DSTATE = 16


def mamba_host_inputs(resid2_bf16, A_log, Dv, W_x, W_dt, b_dt):
    def packcol(b):
        return np.ascontiguousarray(np.asarray(b).reshape(-1, 128).T.astype(np.float32))
    asel = np.full((NT * NG, 128, 128), -1e30, np.float32)
    al = np.asarray(A_log)
    for TT in range(NT):
        for g in range(NG):
            for j in range(8):
                d = TT * 128 + g * 8 + j
                asel[TT * NG + g, g * 8 + j, j * 16:(j + 1) * 16] = al[d, :]
    usel = np.zeros((NG, 128, 128), np.float32)
    for g in range(NG):
        for j in range(8):
            usel[g, g * 8 + j, j * 16:(j + 1) * 16] = 1.0
    bcsel = np.zeros((16, 128), np.float32)
    for n in range(16):
        for j in range(8):
            bcsel[n, j * 16 + n] = 1.0
    nsum = np.zeros((NG, 128, 128), np.float32)
    for g in range(NG):
        for j in range(8):
            for n in range(16):
                nsum[g, j * 16 + n, g * 8 + j] = 1.0
    return {
        "resid2": resid2_bf16,
        "wxT": np.ascontiguousarray(np.asarray(W_x).T),
        "wdtT": np.ascontiguousarray(np.asarray(W_dt).T),
        "bdtt": packcol(b_dt),
        "Dt": packcol(Dv),
        "aselog": asel, "uselm": usel, "bcselm": bcsel, "nsumm": nsum,
    }


def build_mamba_body(nc, rin_d, wxT_d, wdtT_d, bdtt_d, Dt_d, asel_d, usel_d,
                     bcsel_d, nsum_d, out_d):
    ctx = nc.ctx
    gd = Sem(nc, "gd"); sd = Sem(nc, "sd")
    sLNsq = Sem(nc, "sLNsq"); sLNq = Sem(nc, "sLNq")
    sLNv = Sem(nc, "sLNv"); sLNs = Sem(nc, "sLNs")
    gdx = [Sem(nc, "gdx0"), Sem(nc, "gdx1")]
    sXP = Sem(nc, "sXP")
    sXE = Sem(nc, "sXE")
    sDT = Sem(nc, "sDT")
    sDTs = Sem(nc, "sDTs")
    sDTv = Sem(nc, "sDTv")
    sTA = Sem(nc, "sTA")
    sTU = Sem(nc, "sTU")
    sVB = Sem(nc, "sVB")
    sVS = Sem(nc, "sVS")
    sVP = Sem(nc, "sVP")
    sNY = Sem(nc, "sNY")
    sYE = Sem(nc, "sYE")
    sAE = Sem(nc, "sAE")
    sABe = Sem(nc, "sABe")
    sABv = Sem(nc, "sABv")

    xln = ctx.enter_context(nc.sbuf_tensor("xlnmm", [128, NT * CH], BF16))
    rbf = ctx.enter_context(nc.sbuf_tensor("rbfmm", [128, NT * CH], BF16))
    ym = ctx.enter_context(nc.sbuf_tensor("ymm", [128, NT * CH], F32))
    asel = ctx.enter_context(nc.sbuf_tensor("aselb", [128, NT * NG * 128], BF16))
    uselb = ctx.enter_context(nc.sbuf_tensor("uselb", [128, NG * 128], BF16))
    bcselb = ctx.enter_context(nc.sbuf_tensor("bcselb", [16, 128], BF16))
    nsumb = ctx.enter_context(nc.sbuf_tensor("nsumb", [128, NG * 128], BF16))
    wx = ctx.enter_context(nc.sbuf_tensor("wxsb", [128, NT * 96], BF16))
    wdt = ctx.enter_context(nc.sbuf_tensor("wdtsb", [64, D], BF16))
    bdtt = ctx.enter_context(nc.sbuf_tensor("bdtt_sb", [128, 16], F32))
    Dts = ctx.enter_context(nc.sbuf_tensor("Dt_sb", [128, 16], F32))
    dp = ctx.enter_context(nc.sbuf_tensor("dp_sb", [64, CH], BF16))
    Bbf = ctx.enter_context(nc.sbuf_tensor("Bbf", [16, CH], BF16))
    Cbf = ctx.enter_context(nc.sbuf_tensor("Cbf", [16, CH], BF16))
    Brep = ctx.enter_context(nc.sbuf_tensor("Brep", [128, CH], BF16))
    Crep = ctx.enter_context(nc.sbuf_tensor("Crep", [128, CH], BF16))
    delta = ctx.enter_context(nc.sbuf_tensor("delta_sb", [128, CH], BF16))
    ub = ctx.enter_context(nc.sbuf_tensor("ub", [128, CH], BF16))
    tabs = ctx.enter_context(nc.sbuf_tensor("tabs", [128, CH], F32))
    texp = ctx.enter_context(nc.sbuf_tensor("texp", [128, CH], F32))
    trelu = ctx.enter_context(nc.sbuf_tensor("trelu", [128, CH], F32))
    bb = ctx.enter_context(nc.sbuf_tensor("bb", [128, CH], F32))
    ab = ctx.enter_context(nc.sbuf_tensor("ab", [128, CH], F32))
    hb = ctx.enter_context(nc.sbuf_tensor("hb", [128, CH], F32))
    pb = ctx.enter_context(nc.sbuf_tensor("pb", [128, CH], BF16))
    carry = ctx.enter_context(nc.sbuf_tensor("carry", [128, NT * NG], F32))
    ybf = ctx.enter_context(nc.sbuf_tensor("ybfm", [128, CH], BF16))
    xres2 = ctx.enter_context(nc.sbuf_tensor("xres2mm", [128, 2 * CH], BF16))
    sqb = ctx.enter_context(nc.sbuf_tensor("sqbmm", [128, 2 * CH], BF16))
    ones128 = ctx.enter_context(nc.sbuf_tensor("ones128mm", [128, 1], BF16))
    onesrow = ctx.enter_context(nc.sbuf_tensor("onesrowmm", [1, 128], F32))
    eps1 = ctx.enter_context(nc.sbuf_tensor("eps1mm", [1, 1], F32))
    one1 = ctx.enter_context(nc.sbuf_tensor("one1mm", [128, 1], F32))
    aslog = ctx.enter_context(nc.sbuf_tensor("aslog", [128, 2 * 128], F32))
    asexp = ctx.enter_context(nc.sbuf_tensor("asexp", [128, 2 * 128], F32))

    class _L: pass
    ln = _L()
    ln.m = ctx.enter_context(nc.sbuf_tensor("ln_m", [1, CH], F32))
    ln.mv = ctx.enter_context(nc.sbuf_tensor("ln_mv", [1, CH], F32))
    ln.lv = ln.mv; ln.rstd = ln.mv
    ln.t1 = ctx.enter_context(nc.sbuf_tensor("ln_t1", [128, CH], F32))

    P = [ctx.enter_context(nc.psum_tensor(f"P{i}", [128, CH], F32))
         for i in range(8)]

    def emit_ln2(x_get, out_get, xcast_needed=False, resid_dram=None,
                 out2_get=None):
        b_sq, b_q, b_c = sLNsq.n, sLNq.n, sLNv.n
        b_sd0 = sd.n
        with nc.Block() as blk:
            if xcast_needed:
                @blk.vector
                def _(v):
                    v.wait_ge(sd.h, b_sd0)
                    for r in range(NT):
                        sLNv.inc(v.tensor_copy(rbf[:, r * CH:(r + 1) * CH], x_get(r)))
            @blk.scalar
            def _(s):
                for r in range(NT):
                    buf = sqb[:, (r % 2) * CH:(r % 2) * CH + CH]
                    if r >= 2:
                        s.wait_ge(sLNq.h, b_q + r - 1)
                    sLNsq.inc(s.activation(buf, x_get(r), AF.Square))
            @blk.tensor
            def _(tns):
                for r in range(NT):
                    if xcast_needed:
                        tns.wait_ge(sLNv.h, b_c + r + 1)
                        xs = rbf[:, r * CH:(r + 1) * CH]
                    else:
                        xs = x_get(r)
                    tns.matmul(P[0][0:1, :], ones128[:, :], xs,
                               start=(r == 0), stop=(r == NT - 1))
                for r in range(NT):
                    buf = sqb[:, (r % 2) * CH:(r % 2) * CH + CH]
                    tns.wait_ge(sLNsq.h, b_sq + r + 1)
                    sLNq.inc(tns.matmul(P[1][0:1, :], ones128[:, :], buf,
                                        start=(r == 0), stop=(r == NT - 1)))
        b_v, b_s = sLNv.n, sLNs.n
        with nc.Block() as blk:
            @blk.vector
            def _(v):
                sLNv.inc(v.tensor_scalar_mul(ln.m[:, :], P[0][0:1, :], 1.0 / D))
                v.wait_ge(sLNv.h, b_v + 1)
                sLNv.inc(v.tensor_tensor(ln.mv[:, :], ln.m[:, :], ln.m[:, :], ALU.mult))
                v.wait_ge(sLNv.h, b_v + 2)
                sLNv.inc(v.scalar_tensor_tensor(ln.mv[:, :], P[1][0:1, :], 1.0 / D,
                                                ln.mv[:, :], ALU.mult, ALU.subtract))
            @blk.scalar
            def _(s):
                s.wait_ge(sLNv.h, b_v + 3)
                sLNs.inc(s.activation(ln.lv[:, :], ln.mv[:, :], AF.Ln, bias=eps1[0:1, 0:1]))
                s.wait_ge(sLNs.h, b_s + 1)
                sLNs.inc(s.activation(ln.rstd[:, :], ln.lv[:, :], AF.Exp, scale=-0.5))
        with nc.Block() as blk:
            @blk.tensor
            def _(tns):
                tns.matmul(P[2][:, :], onesrow[:, :], ln.m[:, :], start=True, stop=True)
                tns.matmul(P[3][:, :], onesrow[:, :], ln.rstd[:, :], start=True, stop=True)
        b_v2 = sLNv.n
        b_x = [gdx[0].n, gdx[1].n]
        with nc.Block() as blk:
            if resid_dram is not None:
                xd, xt0 = resid_dram
                @blk.gpsimd
                def _(g):
                    for r in range(NT):
                        if r >= 2:
                            g.wait_ge(sLNv.h, b_v2 + 3 * (r - 1))
                        srcv = xd.ap()[r * 128:(r + 1) * 128, xt0:xt0 + CH]
                        gdx[r % 2].dma(g.dma_start(
                            xres2[:, (r % 2) * CH:(r % 2) * CH + CH], srcv))
            @blk.vector
            def _(v):
                nv = 3 if resid_dram is not None else 2
                for r in range(NT):
                    if r >= 1:
                        v.wait_ge(sLNv.h, b_v2 + nv * r)
                    sLNv.inc(v.tensor_tensor(ln.t1[:, :], x_get(r), P[2][:, :],
                                             ALU.subtract))
                    v.wait_ge(sLNv.h, b_v2 + nv * r + 1)
                    sLNv.inc(v.tensor_tensor(out_get(r), ln.t1[:, :], P[3][:, :],
                                             ALU.mult))
                    if resid_dram is not None:
                        v.wait_ge(gdx[r % 2].h, b_x[r % 2] + 16 * (r // 2 + 1))
                        v.wait_ge(sLNv.h, b_v2 + nv * r + 2)
                        sLNv.inc(v.tensor_tensor(out2_get(r), out_get(r),
                                                 xres2[:, (r % 2) * CH:(r % 2) * CH + CH],
                                                 ALU.add))

    with nc.Block() as blk:
        @blk.gpsimd
        def _(g):
            gd.dma(g.dma_start(r3(wx[:, :], NT, 96),
                               wxT_d.ap().rearrange("(t p) f -> p t f", p=128)))
            gd.dma(g.dma_start(wdt[:, :], wdtT_d.ap()))
            gd.dma(g.dma_start(bdtt[:, :], bdtt_d.ap()))
            gd.dma(g.dma_start(Dts[:, :], Dt_d.ap()))
            gd.dma(g.dma_start(r3(uselb[:, :], NG, 128),
                               usel_d.ap().rearrange("g p f -> p g f")))
            gd.dma(g.dma_start(bcselb[:, :], bcsel_d.ap()))
            gd.dma(g.dma_start(r3(nsumb[:, :], NG, 128),
                               nsum_d.ap().rearrange("g p f -> p g f")))
            g.memset(ones128[:, :], 1.0)
            g.memset(onesrow[:, :], 1.0)
            g.memset(eps1[:, :], EPS)
            g.memset(one1[:, :], 1.0)
            g.memset(carry[:, :], 0.0)
        @blk.sync
        def _(s):
            s.wait_ge(gd.h, gd.n)

    b_abe, b_abv = sABe.n, sABv.n
    b_sx = [gdx[0].n, gdx[1].n]
    with nc.Block() as blk:
        @blk.gpsimd
        def _(g):
            for i in range(NT * NG):
                if i >= 2:
                    g.wait_ge(sABe.h, b_abe + i - 1)
                gdx[i % 2].dma(g.dma_start(aslog[:, (i % 2) * 128:(i % 2) * 128 + 128],
                                           asel_d.ap()[i, :, :]))
        @blk.scalar
        def _(s):
            for i in range(NT * NG):
                s.wait_ge(gdx[i % 2].h, b_sx[i % 2] + 16 * (i // 2 + 1))
                if i >= 2:
                    s.wait_ge(sABv.h, b_abv + i - 1)
                sABe.inc(s.activation(asexp[:, (i % 2) * 128:(i % 2) * 128 + 128],
                                      aslog[:, (i % 2) * 128:(i % 2) * 128 + 128],
                                      AF.Exp))
        @blk.vector
        def _(v):
            for i in range(NT * NG):
                v.wait_ge(sABe.h, b_abe + i + 1)
                sABv.inc(v.tensor_scalar_mul(asel[:, i * 128:(i + 1) * 128],
                                             asexp[:, (i % 2) * 128:(i % 2) * 128 + 128],
                                             -1.0))

    for cc in range(4):
        t0 = cc * CH
        b_sd1 = sd.n
        with nc.Block() as blk:
            @blk.gpsimd
            def _(g):
                g.wait_ge(sd.h, b_sd1)
                src = rin_d.ap().rearrange("(t p) f -> p t f", p=128)[:, 0:NT, t0:t0 + CH]
                gd.dma(g.dma_start(r3(rbf[:, :], NT, CH), src))
            @blk.sync
            def _(s):
                s.wait_ge(gd.h, gd.n)
        emit_ln2(lambda r: rbf[:, r * CH:(r + 1) * CH],
                 lambda r: xln[:, r * CH:(r + 1) * CH])

        b_xp, b_xe = sXP.n, sXE.n
        with nc.Block() as blk:
            @blk.tensor
            def _(tns):
                for r in range(NT):
                    mm = tns.matmul(P[4][0:64, :], wx[:, r * 96:r * 96 + 64],
                                    xln[:, r * CH:(r + 1) * CH],
                                    start=(r == 0), stop=(r == NT - 1))
                sXP.inc(mm)
                for r in range(NT):
                    mm = tns.matmul(P[5][0:16, :], wx[:, r * 96 + 64:r * 96 + 80],
                                    xln[:, r * CH:(r + 1) * CH],
                                    start=(r == 0), stop=(r == NT - 1))
                sXP.inc(mm)
                for r in range(NT):
                    mm = tns.matmul(P[6][0:16, :], wx[:, r * 96 + 80:r * 96 + 96],
                                    xln[:, r * CH:(r + 1) * CH],
                                    start=(r == 0), stop=(r == NT - 1))
                sXP.inc(mm)
            @blk.scalar
            def _(s):
                s.wait_ge(sXP.h, b_xp + 1)
                sXE.inc(s.copy(dp[:, :], P[4][0:64, :]))
                s.wait_ge(sXP.h, b_xp + 2)
                sXE.inc(s.copy(Bbf[:, :], P[5][0:16, :]))
                s.wait_ge(sXP.h, b_xp + 3)
                sXE.inc(s.copy(Cbf[:, :], P[6][0:16, :]))
        b_xp2, b_xe2 = sXP.n, sXE.n
        with nc.Block() as blk:
            @blk.tensor
            def _(tns):
                sXP.inc(tns.matmul(P[5][:, :], bcselb[:, :], Bbf[:, :],
                                   start=True, stop=True, skip_group_check=True))
                sXP.inc(tns.matmul(P[6][:, :], bcselb[:, :], Cbf[:, :],
                                   start=True, stop=True, skip_group_check=True))
            @blk.scalar
            def _(s):
                s.wait_ge(sXP.h, b_xp2 + 1)
                sXE.inc(s.copy(Brep[:, :], P[5][:, :]))
                s.wait_ge(sXP.h, b_xp2 + 2)
                sXE.inc(s.copy(Crep[:, :], P[6][:, :]))

        for TT in range(NT):
            b_dt, b_dts, b_dtv = sDT.n, sDTs.n, sDTv.n
            with nc.Block() as blk:
                @blk.tensor
                def _(tns):
                    sDT.inc(tns.matmul(P[7][:, :], wdt[:, TT * 128:(TT + 1) * 128],
                                       dp[:, :], start=True, stop=True,
                                       skip_group_check=True))
                @blk.scalar
                def _(s):
                    s.wait_ge(sDT.h, b_dt + 1)
                    sDTs.inc(s.activation(tabs[:, :], P[7][:, :], AF.Abs,
                                          bias=bdtt[:, TT:TT + 1]))
                    s.wait_ge(sDTs.h, b_dts + 1)
                    sDTs.inc(s.activation(texp[:, :], tabs[:, :], AF.Exp, scale=-1.0))
                    s.wait_ge(sDTs.h, b_dts + 2)
                    sDTs.inc(s.activation(texp[:, :], texp[:, :], AF.Ln,
                                          bias=one1[:, 0:1]))
                @blk.vector
                def _(v):
                    v.wait_ge(sDT.h, b_dt + 1)
                    sDTv.inc(v.tensor_scalar(trelu[:, :], P[7][:, :], bdtt[:, TT:TT + 1],
                                             0.0, ALU.add, ALU.max))
                    v.wait_ge(sDTs.h, b_dts + 3)
                    v.wait_ge(sDTv.h, b_dtv + 1)
                    sDTv.inc(v.tensor_tensor(delta[:, :], texp[:, :], trelu[:, :],
                                             ALU.add))
                    v.wait_ge(sDTv.h, b_dtv + 2)
                    sDTv.inc(v.tensor_tensor(ub[:, :], delta[:, :],
                                             xln[:, TT * CH:(TT + 1) * CH], ALU.mult))
            b_ta, b_tu, b_vb, b_vs, b_vp, b_ny = sTA.n, sTU.n, sVB.n, sVS.n, sVP.n, sNY.n
            b_ae = sAE.n
            with nc.Block() as blk:
                @blk.tensor
                def _(tns):
                    for g in range(NG):
                        gi = TT * NG + g
                        if g >= 1:
                            tns.wait_ge(sAE.h, b_ae + g)
                        sTA.inc(tns.matmul(P[4][:, :], asel[:, gi * 128:(gi + 1) * 128],
                                           delta[:, :], start=True, stop=True,
                                           skip_group_check=True))
                        if g >= 1:
                            tns.wait_ge(sVB.h, b_vb + g)
                        sTU.inc(tns.matmul(P[5][:, :], uselb[:, g * 128:(g + 1) * 128],
                                           ub[:, :], start=True, stop=True,
                                           skip_group_check=True))
                        tns.wait_ge(sVP.h, b_vp + 2 * g + 1)
                        sNY.inc(tns.matmul(P[6][:, :], nsumb[:, g * 128:(g + 1) * 128],
                                           pb[:, :], start=(g == 0), stop=(g == NG - 1),
                                           skip_group_check=True))
                @blk.scalar
                def _(s):
                    for g in range(NG):
                        s.wait_ge(sTA.h, b_ta + g + 1)
                        if g >= 1:
                            s.wait_ge(sVS.h, b_vs + g)
                        sAE.inc(s.activation(ab[:, :], P[4][:, :], AF.Exp))
                @blk.vector
                def _(v):
                    for g in range(NG):
                        gi = TT * NG + g
                        v.wait_ge(sTU.h, b_tu + g + 1)
                        if g >= 1:
                            v.wait_ge(sVS.h, b_vs + g)
                        sVB.inc(v.tensor_tensor(bb[:, :], P[5][:, :], Brep[:, :],
                                                ALU.mult))
                        v.wait_ge(sAE.h, b_ae + g + 1)
                        v.wait_ge(sVB.h, b_vb + g + 1)
                        if g >= 1:
                            v.wait_ge(sVP.h, b_vp + 2 * g - 1)
                        sVS.inc(v.tensor_tensor_scan(hb[:, :], ab[:, :], bb[:, :],
                                                     carry[:, gi:gi + 1],
                                                     ALU.mult, ALU.add))
                        v.wait_ge(sVS.h, b_vs + g + 1)
                        if g >= 1:
                            v.wait_ge(sNY.h, b_ny + g)
                        sVP.inc(v.tensor_tensor(pb[:, :], hb[:, :], Crep[:, :],
                                                ALU.mult))
                        sVP.inc(v.tensor_copy(carry[:, gi:gi + 1], hb[:, CH - 1:CH]))
            b_ny2, b_ye = sNY.n, sYE.n
            with nc.Block() as blk:
                @blk.vector
                def _(v):
                    sYE.inc(v.scalar_tensor_tensor(
                        ym[:, TT * CH:(TT + 1) * CH],
                        xln[:, TT * CH:(TT + 1) * CH], Dts[:, TT:TT + 1],
                        P[6][:, :], ALU.mult, ALU.add))

        emit_ln2(lambda r: ym[:, r * CH:(r + 1) * CH],
                 lambda r: ybf[:, :], xcast_needed=True,
                 resid_dram=(rin_d, t0),
                 out2_get=lambda r: rbf[:, r * CH:(r + 1) * CH])
        with nc.Block() as blk:
            @blk.sync
            def _(s):
                dst = out_d.ap().rearrange("(t p) f -> p t f", p=128)[:, 0:NT, t0:t0 + CH]
                sd.dma(s.dma_start(dst, r3(rbf[:, :], NT, CH)))

    with nc.Block() as blk:
        @blk.sync
        def _(s):
            s.wait_ge(sd.h, sd.n)
    return nc


def build_mamba():
    nc = bass.Bass("TRN2", target_bir_lowering=False)
    rin_d = nc.dram_tensor("resid2", [D, T], BF16, kind="ExternalInput")
    wxT_d = nc.dram_tensor("wxT", [D, 96], F32, kind="ExternalInput")
    wdtT_d = nc.dram_tensor("wdtT", [64, D], F32, kind="ExternalInput")
    bdtt_d = nc.dram_tensor("bdtt", [128, 16], F32, kind="ExternalInput")
    Dt_d = nc.dram_tensor("Dt", [128, 16], F32, kind="ExternalInput")
    asel_d = nc.dram_tensor("aselog", [NT * NG, 128, 128], F32, kind="ExternalInput")
    usel_d = nc.dram_tensor("uselm", [NG, 128, 128], F32, kind="ExternalInput")
    bcsel_d = nc.dram_tensor("bcselm", [16, 128], F32, kind="ExternalInput")
    nsum_d = nc.dram_tensor("nsumm", [NG, 128, 128], F32, kind="ExternalInput")
    out_d = nc.dram_tensor("resid3", [D, T], BF16, kind="ExternalOutput")
    build_mamba_body(nc, rin_d, wxT_d, wdtT_d, bdtt_d, Dt_d, asel_d, usel_d,
                     bcsel_d, nsum_d, out_d)
    return nc


BH = 5632
NBH = 44


def bitnet_host_inputs(resid3_bf16, w1, w2, w3):
    return {
        "resid3": resid3_bf16,
        "bw1T": np.ascontiguousarray(np.asarray(w1).T),
        "bw2T": np.ascontiguousarray(np.asarray(w2).T),
        "bw3T": np.ascontiguousarray(np.asarray(w3).T),
        "identm": np.eye(128, dtype=np.float32),
    }


def build_bitnet_body(nc, rin_d, w1T_d, w2T_d, w3T_d, idn_d, out_d,
                      q1_d, q2_d, q3_d):
    ctx = nc.ctx
    gd = Sem(nc, "gd"); sd = Sem(nc, "sd")
    sLNsq = Sem(nc, "sLNsq"); sLNq = Sem(nc, "sLNq")
    sLNv = Sem(nc, "sLNv"); sLNs = Sem(nc, "sLNs")
    gdx = [Sem(nc, "gdx0"), Sem(nc, "gdx1")]
    gq = [Sem(nc, "gq0"), Sem(nc, "gq1")]
    gqs = [Sem(nc, "gqs0"), Sem(nc, "gqs1")]
    sGA = Sem(nc, "sGA")
    sGC = Sem(nc, "sGC")
    sQs = Sem(nc, "sQs")
    sQv = Sem(nc, "sQv")
    gw13 = [Sem(nc, "gw13_0"), Sem(nc, "gw13_1")]
    gw2 = [Sem(nc, "gw2_0"), Sem(nc, "gw2_1")]
    sZ = Sem(nc, "sZ"); sSi = Sem(nc, "sSi"); sH = Sem(nc, "sH")
    sY = Sem(nc, "sY"); sYE = Sem(nc, "sYE")

    xln = ctx.enter_context(nc.sbuf_tensor("xlnb", [128, NT * CH], BF16))
    rbf = ctx.enter_context(nc.sbuf_tensor("rbfb", [128, NT * CH], BF16))
    hid = ctx.enter_context(nc.sbuf_tensor("hidb", [128, NBH * CH], BF16))
    bno = ctx.enter_context(nc.sbuf_tensor("bno", [128, NT * CH], F32))
    wfb = ctx.enter_context(nc.sbuf_tensor("wfb", [128, 2 * CH], F32))
    sgb = ctx.enter_context(nc.sbuf_tensor("sgb", [128, 2 * CH], BF16))
    sqq = ctx.enter_context(nc.sbuf_tensor("sqq", [128, 2 * CH], F32))
    m01 = ctx.enter_context(nc.sbuf_tensor("m01", [128, 2 * CH], BF16))
    qob = ctx.enter_context(nc.sbuf_tensor("qob", [128, 2 * CH], BF16))
    gpart = ctx.enter_context(nc.sbuf_tensor("gpart", [128, 176], F32))
    grow = ctx.enter_context(nc.sbuf_tensor("grow", [1, 128], F32))
    gsc = ctx.enter_context(nc.sbuf_tensor("gsc", [1, 4], F32))
    ginv = ctx.enter_context(nc.sbuf_tensor("ginv", [128, 1], F32))
    q13 = ctx.enter_context(nc.sbuf_tensor("q13", [128, 2 * 2048], BF16))
    q23 = ctx.enter_context(nc.sbuf_tensor("q23", [128, 2 * 2048], BF16))
    q33 = ctx.enter_context(nc.sbuf_tensor("q33", [128, 2 * NBH * 128], BF16))
    sgm = ctx.enter_context(nc.sbuf_tensor("sgm", [128, CH], F32))
    oro = ctx.enter_context(nc.sbuf_tensor("oro", [128, 2 * CH], F32))
    ybf = ctx.enter_context(nc.sbuf_tensor("ybfb", [128, CH], BF16))
    xres2 = ctx.enter_context(nc.sbuf_tensor("xres2b", [128, 2 * CH], BF16))
    sqb = ctx.enter_context(nc.sbuf_tensor("sqbb", [128, 2 * CH], BF16))
    ones128 = ctx.enter_context(nc.sbuf_tensor("ones128b", [128, 1], BF16))
    onesrow = ctx.enter_context(nc.sbuf_tensor("onesrowb", [1, 128], F32))
    eps1 = ctx.enter_context(nc.sbuf_tensor("eps1b", [1, 1], F32))
    identf = ctx.enter_context(nc.sbuf_tensor("identfb", [128, 128], F32))

    class _L: pass
    ln = _L()
    ln.m = ctx.enter_context(nc.sbuf_tensor("ln_m", [1, CH], F32))
    ln.mv = ctx.enter_context(nc.sbuf_tensor("ln_mv", [1, CH], F32))
    ln.lv = ln.mv; ln.rstd = ln.mv
    ln.t1 = ctx.enter_context(nc.sbuf_tensor("ln_t1", [128, CH], F32))

    P = [ctx.enter_context(nc.psum_tensor(f"P{i}", [128, CH], F32))
         for i in range(8)]

    def emit_ln2(x_get, out_get, xcast_needed=False, resid_dram=None,
                 out2_get=None, out2_dma=None):
        b_sq, b_q, b_c = sLNsq.n, sLNq.n, sLNv.n
        b_sd0 = sd.n
        with nc.Block() as blk:
            if xcast_needed:
                @blk.vector
                def _(v):
                    v.wait_ge(sd.h, b_sd0)
                    for r in range(NT):
                        sLNv.inc(v.tensor_copy(rbf[:, r * CH:(r + 1) * CH], x_get(r)))
            @blk.scalar
            def _(s):
                for r in range(NT):
                    buf = sqb[:, (r % 2) * CH:(r % 2) * CH + CH]
                    if r >= 2:
                        s.wait_ge(sLNq.h, b_q + r - 1)
                    sLNsq.inc(s.activation(buf, x_get(r), AF.Square))
            @blk.tensor
            def _(tns):
                for r in range(NT):
                    if xcast_needed:
                        tns.wait_ge(sLNv.h, b_c + r + 1)
                        xs = rbf[:, r * CH:(r + 1) * CH]
                    else:
                        xs = x_get(r)
                    tns.matmul(P[0][0:1, :], ones128[:, :], xs,
                               start=(r == 0), stop=(r == NT - 1))
                for r in range(NT):
                    buf = sqb[:, (r % 2) * CH:(r % 2) * CH + CH]
                    tns.wait_ge(sLNsq.h, b_sq + r + 1)
                    sLNq.inc(tns.matmul(P[1][0:1, :], ones128[:, :], buf,
                                        start=(r == 0), stop=(r == NT - 1)))
        b_v, b_s = sLNv.n, sLNs.n
        with nc.Block() as blk:
            @blk.vector
            def _(v):
                sLNv.inc(v.tensor_scalar_mul(ln.m[:, :], P[0][0:1, :], 1.0 / D))
                v.wait_ge(sLNv.h, b_v + 1)
                sLNv.inc(v.tensor_tensor(ln.mv[:, :], ln.m[:, :], ln.m[:, :], ALU.mult))
                v.wait_ge(sLNv.h, b_v + 2)
                sLNv.inc(v.scalar_tensor_tensor(ln.mv[:, :], P[1][0:1, :], 1.0 / D,
                                                ln.mv[:, :], ALU.mult, ALU.subtract))
            @blk.scalar
            def _(s):
                s.wait_ge(sLNv.h, b_v + 3)
                sLNs.inc(s.activation(ln.lv[:, :], ln.mv[:, :], AF.Ln, bias=eps1[0:1, 0:1]))
                s.wait_ge(sLNs.h, b_s + 1)
                sLNs.inc(s.activation(ln.rstd[:, :], ln.lv[:, :], AF.Exp, scale=-0.5))
        with nc.Block() as blk:
            @blk.tensor
            def _(tns):
                tns.matmul(P[2][:, :], onesrow[:, :], ln.m[:, :], start=True, stop=True)
                tns.matmul(P[3][:, :], onesrow[:, :], ln.rstd[:, :], start=True, stop=True)
        b_v2 = sLNv.n
        b_x = [gdx[0].n, gdx[1].n]
        b_od = [gqs[0].n, gqs[1].n]
        with nc.Block() as blk:
            if resid_dram is not None:
                xd, xt0 = resid_dram
                @blk.gpsimd
                def _(g):
                    for r in range(NT):
                        if r >= 2:
                            g.wait_ge(sLNv.h, b_v2 + 3 * (r - 1))
                        srcv = xd.ap()[r * 128:(r + 1) * 128, xt0:xt0 + CH]
                        gdx[r % 2].dma(g.dma_start(
                            xres2[:, (r % 2) * CH:(r % 2) * CH + CH], srcv))
            @blk.vector
            def _(v):
                nv = 3 if resid_dram is not None else 2
                for r in range(NT):
                    if r >= 1:
                        v.wait_ge(sLNv.h, b_v2 + nv * r)
                    sLNv.inc(v.tensor_tensor(ln.t1[:, :], x_get(r), P[2][:, :],
                                             ALU.subtract))
                    v.wait_ge(sLNv.h, b_v2 + nv * r + 1)
                    sLNv.inc(v.tensor_tensor(out_get(r), ln.t1[:, :], P[3][:, :],
                                             ALU.mult))
                    if resid_dram is not None:
                        v.wait_ge(gdx[r % 2].h, b_x[r % 2] + 16 * (r // 2 + 1))
                        v.wait_ge(sLNv.h, b_v2 + nv * r + 2)
                        if out2_dma is not None:
                            v.wait_ge(gqs[r % 2].h, b_od[r % 2] + 16 * (r // 2))
                        sLNv.inc(v.tensor_tensor(out2_get(r), out_get(r),
                                                 xres2[:, (r % 2) * CH:(r % 2) * CH + CH],
                                                 ALU.add))
            if out2_dma is not None:
                @blk.sync
                def _(s):
                    for r in range(NT):
                        s.wait_ge(sLNv.h, b_v2 + 3 * (r + 1))
                        gqs[r % 2].dma(s.dma_start(out2_dma(r), out2_get(r)))

    with nc.Block() as blk:
        @blk.gpsimd
        def _(g):
            gd.dma(g.dma_start(identf[:, :], idn_d.ap()))
            g.memset(ones128[:, :], 1.0)
            g.memset(onesrow[:, :], 1.0)
            g.memset(eps1[:, :], EPS)
        @blk.sync
        def _(s):
            s.wait_ge(gd.h, gd.n)

    def emit_quant(w_d, q_d, nrow_tiles, ncol):
        ntile = nrow_tiles * (ncol // CH)
        b_q0 = [gq[0].n, gq[1].n]
        b_ga = sGA.n
        with nc.Block() as blk:
            @blk.gpsimd
            def _(g):
                i = 0
                for rt in range(nrow_tiles):
                    for ct in range(ncol // CH):
                        if i >= 2:
                            g.wait_ge(sGA.h, b_ga + i - 1)
                        gq[i % 2].dma(g.dma_start(
                            wfb[:, (i % 2) * CH:(i % 2) * CH + CH],
                            w_d.ap()[rt * 128:(rt + 1) * 128, ct * CH:(ct + 1) * CH]))
                        i += 1
            @blk.vector
            def _(v):
                i = 0
                for rt in range(nrow_tiles):
                    for ct in range(ncol // CH):
                        v.wait_ge(gq[i % 2].h, b_q0[i % 2] + 16 * (i // 2 + 1))
                        sGA.inc(v.reduce_sum(gpart[:, i:i + 1],
                                             wfb[:, (i % 2) * CH:(i % 2) * CH + CH],
                                             axis=AXX, apply_absolute_value=True))
                        i += 1
        b_gc = sGC.n
        with nc.Block() as blk:
            @blk.vector
            def _(v):
                sGC.inc(v.reduce_sum(ginv[:, 0:1], gpart[:, 0:ntile], axis=AXX))
        with nc.Block() as blk:
            @blk.tensor
            def _(tns):
                tns.transpose(grow[0:1, :].rearrange("a b -> a b") if False else grow[0:1, :],
                              ginv[:, 0:1], identf[:, :])
        return ntile

    def emit_quant2(w_d, q_d, nrow_tiles, ncol):
        ntile_cols = ncol // CH
        ntile = nrow_tiles * ntile_cols
        b_q0 = [gq[0].n, gq[1].n]
        b_ga = sGA.n
        b_qsE = sQs.n
        with nc.Block() as blk:
            @blk.gpsimd
            def _(g):
                g.wait_ge(sQs.h, b_qsE)
                i = 0
                for rt in range(nrow_tiles):
                    for ct in range(ntile_cols):
                        if i >= 2:
                            g.wait_ge(sGA.h, b_ga + i - 1)
                        gq[i % 2].dma(g.dma_start(
                            wfb[:, (i % 2) * CH:(i % 2) * CH + CH],
                            w_d.ap()[rt * 128:(rt + 1) * 128, ct * CH:(ct + 1) * CH]))
                        i += 1
            @blk.vector
            def _(v):
                i = 0
                for rt in range(nrow_tiles):
                    for ct in range(ntile_cols):
                        v.wait_ge(gq[i % 2].h, b_q0[i % 2] + 16 * (i // 2 + 1))
                        sGA.inc(v.reduce_sum(gpart[:, i:i + 1],
                                             wfb[:, (i % 2) * CH:(i % 2) * CH + CH],
                                             axis=AXX, apply_absolute_value=True))
                        i += 1
        b_gc = sGC.n
        with nc.Block() as blk:
            @blk.vector
            def _(v):
                sGC.inc(v.reduce_sum(ginv[:, 0:1], gpart[:, 0:ntile], axis=AXX))
        with nc.Block() as blk:
            @blk.tensor
            def _(tns):
                tns.transpose(P[7][0:1, 0:128], ginv[:, 0:1], identf[:, :])
        with nc.Block() as blk:
            @blk.scalar
            def _(s):
                s.copy(grow[0:1, :], P[7][0:1, 0:128])
        with nc.Block() as blk:
            @blk.vector
            def _(v):
                b = sGC.n
                sGC.inc(v.reduce_sum(gsc[0:1, 0:1], grow[0:1, :], axis=AXX))
                v.wait_ge(sGC.h, b + 1)
                sGC.inc(v.tensor_scalar(gsc[0:1, 1:2], gsc[0:1, 0:1],
                                        1.0 / (nrow_tiles * 128 * ncol), 1e-8,
                                        ALU.mult, ALU.add))
                v.wait_ge(sGC.h, b + 2)
                sGC.inc(v.reciprocal(gsc[0:1, 2:3], gsc[0:1, 1:2]))
        with nc.Block() as blk:
            @blk.tensor
            def _(tns):
                tns.matmul(P[7][:, 0:1], onesrow[:, :], gsc[0:1, 2:3],
                           start=True, stop=True, skip_group_check=True)
        with nc.Block() as blk:
            @blk.scalar
            def _(s):
                s.copy(ginv[:, 0:1], P[7][:, 0:1])
        b_q1 = [gq[0].n, gq[1].n]
        b_qo = [gqs[0].n, gqs[1].n]
        b_qs, b_qv = sQs.n, sQv.n
        b_gaE = sGA.n
        with nc.Block() as blk:
            @blk.gpsimd
            def _(g):
                g.wait_ge(sGA.h, b_gaE)
                i = 0
                for rt in range(nrow_tiles):
                    for ct in range(ntile_cols):
                        if i >= 2:
                            g.wait_ge(sQs.h, b_qs + 2 * (i - 1))
                        gq[i % 2].dma(g.dma_start(
                            wfb[:, (i % 2) * CH:(i % 2) * CH + CH],
                            w_d.ap()[rt * 128:(rt + 1) * 128, ct * CH:(ct + 1) * CH]))
                        i += 1
            @blk.scalar
            def _(s):
                i = 0
                for rt in range(nrow_tiles):
                    for ct in range(ntile_cols):
                        s.wait_ge(gq[i % 2].h, b_q1[i % 2] + 16 * (i // 2 + 1))
                        if i >= 2:
                            s.wait_ge(sQv.h, b_qv + 2 * (i - 1))
                        sQs.inc(s.activation(sgb[:, (i % 2) * CH:(i % 2) * CH + CH],
                                             wfb[:, (i % 2) * CH:(i % 2) * CH + CH],
                                             AF.Sign))
                        sQs.inc(s.activation(sqq[:, (i % 2) * CH:(i % 2) * CH + CH],
                                             wfb[:, (i % 2) * CH:(i % 2) * CH + CH],
                                             AF.Square, scale=ginv[:, 0:1]))
                        i += 1
            @blk.vector
            def _(v):
                i = 0
                for rt in range(nrow_tiles):
                    for ct in range(ntile_cols):
                        v.wait_ge(sQs.h, b_qs + 2 * i + 2)
                        v.wait_ge(gqs[i % 2].h,
                                  b_qo[i % 2] + (16 * (i // 2) if i >= 2 else 0))
                        sQv.inc(v.tensor_single_scalar(
                            m01[:, (i % 2) * CH:(i % 2) * CH + CH],
                            sqq[:, (i % 2) * CH:(i % 2) * CH + CH], 0.25, ALU.is_ge))
                        v.wait_ge(sQv.h, b_qv + 2 * i + 1)
                        sQv.inc(v.tensor_tensor(
                            qob[:, (i % 2) * CH:(i % 2) * CH + CH],
                            sgb[:, (i % 2) * CH:(i % 2) * CH + CH],
                            m01[:, (i % 2) * CH:(i % 2) * CH + CH], ALU.mult))
                        i += 1
            @blk.sync
            def _(s):
                i = 0
                for rt in range(nrow_tiles):
                    for ct in range(ntile_cols):
                        s.wait_ge(sQv.h, b_qv + 2 * i + 2)
                        gqs[i % 2].dma(s.dma_start(
                            q_d.ap()[rt * 128:(rt + 1) * 128, ct * CH:(ct + 1) * CH],
                            qob[:, (i % 2) * CH:(i % 2) * CH + CH]))
                        i += 1

    emit_quant2(w1T_d, q1_d, NT, BH)
    emit_quant2(w2T_d, q2_d, NT, BH)
    emit_quant2(w3T_d, q3_d, NBH, D)

    with nc.Block() as blk:
        @blk.sync
        def _(s):
            s.wait_ge(gqs[0].h, gqs[0].n)
            s.wait_ge(gqs[1].h, gqs[1].n)

    for cc in range(4):
        t0 = cc * CH
        b_sd1 = sd.n
        with nc.Block() as blk:
            @blk.gpsimd
            def _(g):
                src = rin_d.ap().rearrange("(t p) f -> p t f", p=128)[:, 0:NT, t0:t0 + CH]
                gd.dma(g.dma_start(r3(rbf[:, :], NT, CH), src))
            @blk.sync
            def _(s):
                s.wait_ge(gd.h, gd.n)
                s.wait_ge(sd.h, sd.n)
        emit_ln2(lambda r: rbf[:, r * CH:(r + 1) * CH],
                 lambda r: xln[:, r * CH:(r + 1) * CH])

        wtq = {}

        def mkq13(ht, slot):
            def em(g):
                gw13[slot].dma(g.dma_start(
                    r3(q13[:, slot * 2048:slot * 2048 + 2048], NT, 128),
                    col_view(q1_d, ht * 128, 128)))
                gw13[slot].dma(g.dma_start(
                    r3(q23[:, slot * 2048:slot * 2048 + 2048], NT, 128),
                    col_view(q2_d, ht * 128, 128)))
                wtq[("z", ht)] = gw13[slot].n
            return em

        def mkq3(dt, slot2):
            def em(g):
                gw2[slot2].dma(g.dma_start(
                    r3(q33[:, slot2 * NBH * 128:slot2 * NBH * 128 + NBH * 128], NBH, 128),
                    col_view(q3_d, dt * 128, 128, nt=NBH)))
                wtq[("o", dt)] = gw2[slot2].n
            return em

        qsteps = [mkq13(ht, ht % 2) for ht in range(NBH)] + \
                 [mkq3(dt, dt % 2) for dt in range(NT)]
        with nc.Block() as blk:
            @blk.gpsimd
            def _(g):
                qsteps[0](g)

        for ht in range(NBH):
            slot = ht % 2
            b_z, b_si, b_h = sZ.n, sSi.n, sH.n
            with nc.Block() as blk:
                @blk.gpsimd
                def _(g):
                    if ht + 1 < len(qsteps):
                        qsteps[ht + 1](g)
                @blk.tensor
                def _(tns):
                    tns.wait_ge(gw13[slot].h, wtq[("z", ht)])
                    for r in range(NT):
                        tns.matmul(P[4][:, :],
                                   q13[:, slot * 2048 + r * 128: slot * 2048 + (r + 1) * 128],
                                   xln[:, r * CH:(r + 1) * CH],
                                   start=(r == 0), stop=(r == NT - 1))
                    for r in range(NT):
                        mm = tns.matmul(P[5][:, :],
                                        q23[:, slot * 2048 + r * 128: slot * 2048 + (r + 1) * 128],
                                        xln[:, r * CH:(r + 1) * CH],
                                        start=(r == 0), stop=(r == NT - 1))
                    sZ.inc(mm)
                @blk.scalar
                def _(s):
                    s.wait_ge(sZ.h, b_z + 1)
                    sSi.inc(s.activation(sgm[:, :], P[4][:, :], AF.Sigmoid))
                @blk.vector
                def _(v):
                    v.wait_ge(sSi.h, b_si + 1)
                    sH.inc(v.tensor_tensor(hid[:, ht * CH:(ht + 1) * CH],
                                           sgm[:, :], P[5][:, :], ALU.mult))
        for dt in range(NT):
            slot2 = dt % 2
            b_y, b_ye = sY.n, sYE.n
            with nc.Block() as blk:
                @blk.gpsimd
                def _(g):
                    if NBH + dt + 1 < len(qsteps):
                        qsteps[NBH + dt + 1](g)
                @blk.tensor
                def _(tns):
                    tns.wait_ge(gw2[slot2].h, wtq[("o", dt)])
                    for hh in range(NBH):
                        mm = tns.matmul(
                            P[6][:, :],
                            q33[:, slot2 * NBH * 128 + hh * 128: slot2 * NBH * 128 + (hh + 1) * 128],
                            hid[:, hh * CH:(hh + 1) * CH],
                            start=(hh == 0), stop=(hh == NBH - 1),
                            skip_group_check=True)
                    sY.inc(mm)
                @blk.scalar
                def _(s):
                    s.wait_ge(sY.h, b_y + 1)
                    sYE.inc(s.copy(bno[:, dt * CH:(dt + 1) * CH], P[6][:, :]))

        emit_ln2(lambda r: bno[:, r * CH:(r + 1) * CH],
                 lambda r: ybf[:, :], xcast_needed=True,
                 resid_dram=(rin_d, t0),
                 out2_get=lambda r: oro[:, (r % 2) * CH:(r % 2) * CH + CH],
                 out2_dma=lambda r, t0=t0: out_d.ap()[r * 128:(r + 1) * 128, t0:t0 + CH])

    with nc.Block() as blk:
        @blk.sync
        def _(s):
            s.wait_ge(sd.h, sd.n)
            s.wait_ge(gqs[0].h, gqs[0].n)
            s.wait_ge(gqs[1].h, gqs[1].n)
    return nc


def build_bitnet():
    nc = bass.Bass("TRN2", target_bir_lowering=False)
    rin_d = nc.dram_tensor("resid3", [D, T], BF16, kind="ExternalInput")
    w1T_d = nc.dram_tensor("bw1T", [D, BH], F32, kind="ExternalInput")
    w2T_d = nc.dram_tensor("bw2T", [D, BH], F32, kind="ExternalInput")
    w3T_d = nc.dram_tensor("bw3T", [BH, D], F32, kind="ExternalInput")
    idn_d = nc.dram_tensor("identm", [128, 128], F32, kind="ExternalInput")
    out_d = nc.dram_tensor("finout", [D, T], F32, kind="ExternalOutput")
    q1_d = nc.dram_tensor("q1scr", [D, BH], BF16, kind="ExternalOutput")
    q2_d = nc.dram_tensor("q2scr", [D, BH], BF16, kind="ExternalOutput")
    q3_d = nc.dram_tensor("q3scr", [BH, D], BF16, kind="ExternalOutput")
    build_bitnet_body(nc, rin_d, w1T_d, w2T_d, w3T_d, idn_d, out_d,
                      q1_d, q2_d, q3_d)
    return nc


class _NCP:
    def __init__(self, nc, prefix, phase_ctx):
        self._nc = nc
        self._px = prefix
        self.ctx = phase_ctx
        self.sem_ctx = nc.ctx

    def sbuf_tensor(self, name, shape, dtype, **kw):
        return self._nc.sbuf_tensor(self._px + name, shape, dtype, **kw)

    def psum_tensor(self, name, shape, dtype, **kw):
        return self._nc.psum_tensor(self._px + name, shape, dtype, **kw)

    def semaphore(self, name=None, num=None):
        return self._nc.semaphore(self._px + (name or "s"), num)

    def __getattr__(self, k):
        return getattr(self._nc, k)


def build_fused():
    from contextlib import ExitStack
    nc = bass.Bass("TRN2", target_bir_lowering=False)
    x_d = nc.dram_tensor("x_t", [D, T], F32, kind="ExternalInput")
    wqkvT_d = nc.dram_tensor("wqkvT", [D, 3 * D], F32, kind="ExternalInput")
    woutT_d = nc.dram_tensor("woutT", [D, D], F32, kind="ExternalInput")
    cos_d = nc.dram_tensor("cosT", [HD, T], F32, kind="ExternalInput")
    sin_d = nc.dram_tensor("sinT", [HD, T], F32, kind="ExternalInput")
    rot_d = nc.dram_tensor("rotm", [HD, HD], F32, kind="ExternalInput")
    msk_d = nc.dram_tensor("maskblk", [128, 4 * CH], F32, kind="ExternalInput")
    idn_d = nc.dram_tensor("identm", [128, 128], F32, kind="ExternalInput")
    gate_wT_d = nc.dram_tensor("gate_wT", [D, 16], F32, kind="ExternalInput")
    gbrep_d = nc.dram_tensor("gbrep", [128, 16], F32, kind="ExternalInput")
    w1T_d = nc.dram_tensor("w1T", [NE, D, EH], F32, kind="ExternalInput")
    w3T_d = nc.dram_tensor("w3T", [NE, D, EH], F32, kind="ExternalInput")
    w2T_d = nc.dram_tensor("w2T", [NE, EH, D], F32, kind="ExternalInput")
    b1t_d = nc.dram_tensor("b1t", [128, NE * NHT], F32, kind="ExternalInput")
    b3t_d = nc.dram_tensor("b3t", [128, NE * NHT], F32, kind="ExternalInput")
    b2_d = nc.dram_tensor("b2sb", [16, D], F32, kind="ExternalInput")
    ws1T_d = nc.dram_tensor("ws1T", [D, SH], F32, kind="ExternalInput")
    ws3T_d = nc.dram_tensor("ws3T", [D, SH], F32, kind="ExternalInput")
    ws2T_d = nc.dram_tensor("ws2T", [SH, D], F32, kind="ExternalInput")
    bs1t_d = nc.dram_tensor("bs1t", [128, NST], F32, kind="ExternalInput")
    bs3t_d = nc.dram_tensor("bs3t", [128, NST], F32, kind="ExternalInput")
    bs2t_d = nc.dram_tensor("bs2t", [128, NT], F32, kind="ExternalInput")
    esel_d = nc.dram_tensor("eselm", [16, 16 * 128], F32, kind="ExternalInput")
    wxT_d = nc.dram_tensor("wxT", [D, 96], F32, kind="ExternalInput")
    wdtT_d = nc.dram_tensor("wdtT", [64, D], F32, kind="ExternalInput")
    bdtt_d = nc.dram_tensor("bdtt", [128, 16], F32, kind="ExternalInput")
    Dt_d = nc.dram_tensor("Dt", [128, 16], F32, kind="ExternalInput")
    asel_d = nc.dram_tensor("aselog", [NT * NG, 128, 128], F32, kind="ExternalInput")
    usel_d = nc.dram_tensor("uselm", [NG, 128, 128], F32, kind="ExternalInput")
    bcsel_d = nc.dram_tensor("bcselm", [16, 128], F32, kind="ExternalInput")
    nsum_d = nc.dram_tensor("nsumm", [NG, 128, 128], F32, kind="ExternalInput")
    bw1T_d = nc.dram_tensor("bw1T", [D, BH], F32, kind="ExternalInput")
    bw2T_d = nc.dram_tensor("bw2T", [D, BH], F32, kind="ExternalInput")
    bw3T_d = nc.dram_tensor("bw3T", [BH, D], F32, kind="ExternalInput")
    xln1_d = nc.dram_tensor("xln1_scr", [D, T], BF16, kind="ExternalOutput")
    r1_d = nc.dram_tensor("resid1", [D, T], BF16, kind="ExternalOutput")
    r2_d = nc.dram_tensor("resid2", [D, T], BF16, kind="ExternalOutput")
    r3_d = nc.dram_tensor("resid3", [D, T], BF16, kind="ExternalOutput")
    q1_d = nc.dram_tensor("q1scr", [D, BH], BF16, kind="ExternalOutput")
    q2_d = nc.dram_tensor("q2scr", [D, BH], BF16, kind="ExternalOutput")
    q3_d = nc.dram_tensor("q3scr", [BH, D], BF16, kind="ExternalOutput")
    out_d = nc.dram_tensor("finout", [D, T], F32, kind="ExternalOutput")

    with ExitStack() as c1:
        build_attn_body(_NCP(nc, "a_", c1), x_d, wqkvT_d, woutT_d, cos_d,
                        sin_d, rot_d, msk_d, idn_d, r1_d, xln1_d)
    with ExitStack() as c2:
        build_moe_body(_NCP(nc, "m_", c2), r1_d, gate_wT_d, gbrep_d, w1T_d,
                       w3T_d, w2T_d, b1t_d, b3t_d, b2_d, ws1T_d, ws3T_d,
                       ws2T_d, bs1t_d, bs3t_d, bs2t_d, idn_d, esel_d, r2_d)
    with ExitStack() as c3:
        build_mamba_body(_NCP(nc, "s_", c3), r2_d, wxT_d, wdtT_d, bdtt_d,
                         Dt_d, asel_d, usel_d, bcsel_d, nsum_d, r3_d)
    with ExitStack() as c4:
        build_bitnet_body(_NCP(nc, "b_", c4), r3_d, bw1T_d, bw2T_d, bw3T_d,
                          idn_d, out_d, q1_d, q2_d, q3_d)
    return nc


def fused_host_inputs(gi, c):
    im = attn_host_inputs(gi["L0_attn_x"][c], gi["L0_attn_Wqkv"], gi["L0_attn_Wout"])
    m = moe_host_inputs(None, gi["L0_ffn_gate_w"], gi["L0_ffn_gate_b"],
                        gi["L0_ffn_w1"], gi["L0_ffn_b1"], gi["L0_ffn_w2"],
                        gi["L0_ffn_b2"], gi["L0_ffn_w3"], gi["L0_ffn_b3"],
                        gi["L0_ffn_ws1"], gi["L0_ffn_bs1"], gi["L0_ffn_ws2"],
                        gi["L0_ffn_bs2"], gi["L0_ffn_ws3"], gi["L0_ffn_bs3"])
    del m["resid1"]
    im.update(m)
    s = mamba_host_inputs(None, gi["L1_attn_A_log"], gi["L1_attn_D"],
                          gi["L1_attn_W_x"], gi["L1_attn_W_dt"], gi["L1_attn_b_dt"])
    del s["resid2"]
    im.update(s)
    b = bitnet_host_inputs(None, gi["L1_ffn_w1"], gi["L1_ffn_w2"], gi["L1_ffn_w3"])
    del b["resid3"]
    im.update(b)
    return im


def kernel(**inputs):
    gi = {k: np.asarray(v) for k, v in inputs.items()}
    nc = build_fused()
    in_maps = [fused_host_inputs(gi, c) for c in range(8)]
    res = run_bass_kernel_spmd(nc, in_maps, core_ids=list(range(8)))
    out = np.stack([np.ascontiguousarray(
        np.asarray(res.results[c]["finout"]).T) for c in range(8)])
    return out.astype(np.float32)
